# version 29
# baseline (speedup 1.0000x reference)
import collections
import contextlib
import numpy as np
import ml_dtypes
import concourse.bass as bass
import concourse.mybir as mybir
from concourse.bass_utils import run_bass_kernel_spmd

F32 = mybir.dt.float32
BF16 = mybir.dt.bfloat16
AF = mybir.ActivationFunctionType
ALU = mybir.AluOpType
AX = mybir.AxisListType

EPS = 1e-5
C = 1024
KC = 8
T = 4096
ST = 512
NST = T // ST
CH = 4
NSEQ = 2
NCORE = 8

ENGINES = ("sync", "gpsimd", "vector", "scalar", "tensor")


class Prog:

    def __init__(self, nc):
        self.nc = nc
        self.ops = []
        self.res = {}
        self.cnt = collections.defaultdict(int)

    def _st(self, key):
        if key not in self.res:
            self.res[key] = {"w": [], "r": []}
        return self.res[key]

    SELF_ORDERED = set()

    def op(self, eng, fn, reads=(), writes=(), dma=False, tag="", extra_waits=()):
        kind = "dma" if dma else "cmp"
        ev = (eng, kind, self.cnt[(eng, kind)])
        self.cnt[(eng, kind)] += 1
        waits = set(extra_waits)
        for r in reads:
            st = self._st(r)
            waits.update(st["w"])
        for w in writes:
            st = self._st(w)
            waits.update(st["w"])
            waits.update(st["r"])
        waits.discard(ev)
        if (eng, kind) in self.SELF_ORDERED:
            waits = {x for x in waits if (x[0], x[1]) != (eng, kind)}
        self.ops.append(dict(ev=ev, fn=fn, waits=waits, tag=tag))
        for r in reads:
            self._st(r)["r"].append(ev)
        for w in writes:
            self.res[w] = {"w": [ev], "r": []}
        return ev

    def par(self, specs):
        snap = {}
        keys = set()
        for _, _, reads, writes, _ in specs:
            keys |= set(reads) | set(writes)
        for k in keys:
            st = self._st(k)
            snap[k] = (list(st["w"]), list(st["r"]))
        evs = []
        wrote = {}
        for eng, fn, reads, writes, tag in specs:
            kind = "cmp"
            ev = (eng, kind, self.cnt[(eng, kind)])
            self.cnt[(eng, kind)] += 1
            waits = set()
            for r in reads:
                waits.update(snap[r][0])
            for w in writes:
                waits.update(snap[w][0])
                waits.update(snap[w][1])
            waits.discard(ev)
            self.ops.append(dict(ev=ev, fn=fn, waits=waits, tag=tag))
            evs.append(ev)
            for r in reads:
                self._st(r)["r"].append(ev)
            for w in writes:
                wrote.setdefault(w, []).append(ev)
        for w, wl in wrote.items():
            self.res[w] = {"w": list(wl), "r": []}
        return evs

    def last_evs(self):
        return [(e, k, n - 1) for (e, k), n in self.cnt.items() if n > 0]

    def barrier(self, engines=ENGINES):
        evs = self.last_evs()
        for eng in engines:
            self.op(eng, lambda e: e.nop(), extra_waits=evs, tag="barrier")

    def finalize(self):
        evs = self.last_evs()
        self.op("sync", lambda e: e.nop(), extra_waits=evs, tag="final")

    def emit(self, exitstack):
        nc = self.nc
        needed = set()
        for o in self.ops:
            needed |= o["waits"]
        for o in self.ops:
            if o["ev"][1] == "dma":
                needed.add(o["ev"])
        incval = {}
        run = collections.defaultdict(int)
        for o in self.ops:
            eng, kind, _ = o["ev"]
            if o["ev"] in needed:
                run[(eng, kind)] += 16 if kind == "dma" else 1
                incval[o["ev"]] = run[(eng, kind)]
        sems = {}
        for key, total in run.items():
            if total > 0:
                sems[key] = exitstack.enter_context(
                    nc.semaphore(f"s_{key[0]}_{key[1]}"))
        for o in self.ops:
            for w in o["waits"]:
                assert w in incval, (w, o["tag"])
        per_eng = collections.defaultdict(list)
        for o in self.ops:
            per_eng[o["ev"][0]].append(o)
        block = exitstack.enter_context(nc.Block())
        watermark = collections.defaultdict(int)

        warm = getattr(self, "warm", None)

        def runner(eng_name):
            def go(engine):
                for o in per_eng[eng_name]:
                    bysem = {}
                    for w in o["waits"]:
                        k = (w[0], w[1])
                        v = incval[w]
                        if v > bysem.get(k, 0):
                            bysem[k] = v
                    emitted_wait = False
                    for k, v in sorted(bysem.items()):
                        if v > watermark[(eng_name, k)]:
                            if (eng_name == "tensor" and warm is not None
                                    and not emitted_wait):
                                warm(engine)
                                emitted_wait = True
                            engine.wait_ge(sems[k], v)
                            watermark[(eng_name, k)] = v
                    ins = o["fn"](engine)
                    if o["ev"] in incval:
                        eng, kind, _ = o["ev"]
                        ins.then_inc(sems[(eng, kind)], 16 if kind == "dma" else 1)
            return go

        for eng_name in ENGINES:
            if per_eng[eng_name]:
                getattr(block, eng_name)(runner(eng_name))
        return sems


def build_program(debug=False):
    nc = bass.Bass("TRN2", target_bir_lowering=False, debug=False)
    dram = {}

    def din(name, shape, dt=F32):
        dram[name] = nc.dram_tensor(name, shape, dt, kind="ExternalInput").ap()
        return dram[name]

    xT = din("xT", [NSEQ, C, T])
    gates_w = din("gates_w", [C, 3 * C])
    out_w = din("out_w", [C, C])
    w1 = din("w1", [C, C])
    w2 = din("w2", [C, C])
    U1 = din("U1", [C, 256])
    V1 = din("V1", [256, 4096])
    U2 = din("U2", [4096, 256])
    V2 = din("V2", [256, C])
    dn_g = din("dn_g", [128, 8])
    dn_b = din("dn_b", [128, 8])
    mn_g = din("mn_g", [128, 8])
    mn_b = din("mn_b", [128, 8])
    gates_b_col = din("gates_b_col", [128, 24])
    out_b_col = din("out_b_col", [128, 8])
    b1a_col = din("b1a_col", [128, 8])
    b2a_col = din("b2a_col", [128, 8])
    fb1_col = din("fb1_col", [128, 32])
    fb2_col = din("fb2_col", [128, 8])
    lrw_col = din("lrw_col", [128, 8])
    lrb_in = din("lrb_in", [1, 1])
    gates_b_row = din("gates_b_row", [1, 3 * C])
    ident_in = din("ident_in", [128, 128])
    maskL_in = din("maskL_in", [128, 128])
    maskU_in = din("maskU_in", [128, 128])

    yT = nc.dram_tensor("yT", [NSEQ, C, T], F32, kind="ExternalOutput").ap()

    Wstash = nc.dram_tensor("Wstash", [128, NSEQ * 8192], BF16, kind="Internal").ap()
    rows_hbm = nc.dram_tensor("rows_hbm", [1, NSEQ * NST * 2048], BF16, kind="Internal").ap()
    h1_hbm = nc.dram_tensor("h1_hbm", [128, NSEQ * T * KC], F32, kind="Internal").ap()

    taps = {}

    def tap(name, shape, dt=F32):
        if not debug:
            return None
        taps[name] = nc.dram_tensor("tap_" + name, shape, dt, kind="ExternalOutput").ap()
        return taps[name]

    p = Prog(nc)
    ctx = contextlib.ExitStack()
    E = ctx.enter_context

    sb = {}

    def alloc(name, shape, dt, stack=None):
        t = (stack or ctx).enter_context(nc.sbuf_tensor("sb_" + name, shape, dt))
        sb[name] = t
        return t

    for nm, src in (("dn_g", dn_g), ("dn_b", dn_b), ("mn_g", mn_g), ("mn_b", mn_b),
                    ("gb_col", gates_b_col), ("ob_col", out_b_col),
                    ("b1a", b1a_col), ("b2a", b2a_col), ("fb1", fb1_col),
                    ("fb2", fb2_col), ("lrw_f", lrw_col)):
        t = alloc(nm, list(src.shape), F32)
        p.op("sync", lambda e, t=t, src=src: e.dma_start(out=t[:], in_=src),
             writes=[nm], dma=True, tag="ld_" + nm)
    lrb_sb = alloc("lrb", [1, 1], F32)
    p.op("sync", lambda e: e.dma_start(out=lrb_sb[:], in_=lrb_in), writes=["lrb"], dma=True)
    ident_bf = alloc("ident_bf", [128, 128], BF16)
    maskL = alloc("maskL", [128, 128], BF16)
    maskU = alloc("maskU", [128, 128], BF16)
    for t, src, nm in ((ident_bf, ident_in, "ident"), (maskL, maskL_in, "maskL"),
                       (maskU, maskU_in, "maskU")):
        p.op("gpsimd", lambda e, t=t, src=src: e.dma_start(out=t[:], in_=src),
             writes=[nm], dma=True, tag="ld_" + nm)
    ones_col = alloc("ones_col", [128, 1], BF16)
    p.op("gpsimd", lambda e: e.memset(ones_col[:], 1.0), writes=["ones_col"])
    ones_row = alloc("ones_row", [1, 128], BF16)
    p.op("gpsimd", lambda e: e.memset(ones_row[:], 1.0), writes=["ones_row"])
    one1_f = alloc("one1_f", [1, 1], F32)
    p.op("gpsimd", lambda e: e.memset(one1_f[:], 1.0), writes=["one1"])
    eps1 = alloc("eps1", [1, 1], F32)
    p.op("gpsimd", lambda e: e.memset(eps1[:], EPS), writes=["eps1"])
    eps1b = alloc("eps1b", [1, 1], F32)
    p.op("gpsimd", lambda e: e.memset(eps1b[:], 1.0 + EPS), writes=["eps1b"])
    eps2 = alloc("eps2", [1, 1], F32)
    p.op("gpsimd", lambda e: e.memset(eps2[:], EPS * EPS), writes=["eps2"])
    neghalf = alloc("neghalf", [1, 1], F32)
    p.op("gpsimd", lambda e: e.memset(neghalf[:], -0.5), writes=["neghalf"])
    dnb_bf = alloc("dnb_bf", [128, 8], BF16)
    p.op("vector", lambda e: e.tensor_copy(dnb_bf[:], sb["dn_b"][:]),
         reads=["dn_b"], writes=["dnb_bf"])
    mnb_bf = alloc("mnb_bf", [128, 8], BF16)
    p.op("vector", lambda e: e.tensor_copy(mnb_bf[:], sb["mn_b"][:]),
         reads=["mn_b"], writes=["mnb_bf"])
    biasq_col = alloc("biasq_col", [128, 8], F32)
    bias1_col = alloc("bias1_col", [128, 8], F32)
    lrb_col = alloc("lrb_col", [128, 1], F32)

    psA = [E(nc.psum_tensor(f"psA{i}", [128, 512], F32)) for i in range(2)]
    psS = [E(nc.psum_tensor(f"psS{i}", [128, 512], F32)) for i in range(2)]
    psT = [E(nc.psum_tensor(f"psT{i}", [128, 512], F32)) for i in range(2)]
    psB = [E(nc.psum_tensor(f"psB{i}", [128, 512], F32)) for i in range(1)]
    psW = E(nc.psum_tensor("psW", [128, 512], F32))
    ROT4 = [(psA[0], "bA0"), (psA[1], "bA1"), (psB[0], "bB0")]
    ROTS = [(psS[0], "bS0"), (psS[1], "bS1"), (psA[0], "bA0"), (psA[1], "bA1")]

    def _warm(eng):
        eng.matmul(psW[0:1, 0:1], ones_row[0:1, 0:1], ones_row[0:1, 0:1],
                   start=True, stop=True, skip_group_check=True)
    p.warm = None

    def mm_group(bank_res, psum_ap, pairs, reads, tag, start=True, stop=True):
        def fn(e, pairs=pairs, psum_ap=psum_ap):
            ins = None
            n = len(pairs)
            for i, (lhsT, rhs) in enumerate(pairs):
                ins = e.matmul(psum_ap, lhsT, rhs,
                               start=(start and i == 0), stop=(stop and i == n - 1),
                               skip_group_check=True)
            return ins
        return p.op("tensor", fn, reads=list(reads), writes=[bank_res], tag=tag)

    def bcast(row_ap, row_res, tile_ap, tile_res, bank_res, ps, tag):
        p.op("tensor", lambda e: e.matmul(ps[:, 0:512], ones_row[:], row_ap,
                                          start=True, stop=True, skip_group_check=True),
             reads=["ones_row", row_res], writes=[bank_res], tag="bc_" + tag)
        p.op("scalar", lambda e: e.copy(tile_ap, ps[:, 0:512]),
             reads=[bank_res], writes=[tile_res], tag="bce_" + tag)

    rowa = alloc("rowa", [1, 512 * 5], F32)
    R_M, R_E2, R_T, R_RS, R_T2 = (rowa[:][0:1, i * 512:(i + 1) * 512] for i in range(5))
    R_X = R_T2

    def ln_stat_rows(sx_res, sxx_res, ps_sx, ps_sxx, m_out=None, rs_out=None,
                     rs12_out=None, tag=""):
        p.op("vector", lambda e: e.tensor_scalar_mul(R_M, ps_sx[0:1, 0:512], 1.0 / C),
             reads=[sx_res], writes=["r_m"], tag="rm" + tag)
        p.op("vector", lambda e: e.tensor_scalar_mul(R_E2, ps_sxx[0:1, 0:512], 1.0 / C),
             reads=[sxx_res], writes=["r_e2"], tag="re2" + tag)
        p.op("vector", lambda e: e.tensor_tensor(R_T, R_M, R_M, ALU.mult),
             reads=["r_m"], writes=["r_t"], tag="rmsq" + tag)
        p.op("vector", lambda e: e.scalar_tensor_tensor(
            R_T, R_E2, 1.0, R_T, ALU.mult, ALU.subtract),
            reads=["r_e2", "r_t"], writes=["r_t"], tag="rvar" + tag)
        if m_out is not None:
            ap, res = m_out
            p.op("vector", lambda e, ap=ap: e.tensor_copy(ap, R_M),
                 reads=["r_m"], writes=[res], tag="rmo" + tag)
        if rs_out is not None:
            ap, res = rs_out
            p.op("scalar", lambda e: e.activation(R_T2, R_T, AF.Ln, bias=eps1[:]),
                 reads=["r_t", "eps1"], writes=["r_t2"], tag="rln" + tag)
            p.op("scalar", lambda e, ap=ap: e.activation(ap, R_T2, AF.Exp,
                                                         scale=neghalf[:]),
                 reads=["r_t2", "neghalf"], writes=[res], tag="rrs" + tag)
        if rs12_out is not None:
            ap, res = rs12_out
            p.op("scalar", lambda e: e.activation(
                R_X, R_T, AF.Ln, bias=eps2[:], scale=eps1b[:]),
                reads=["r_t", "eps2", "eps1b"], writes=["r_t2"], tag="rln12" + tag)
            p.op("scalar", lambda e, ap=ap: e.activation(ap, R_X, AF.Exp,
                                                         scale=neghalf[:]),
                 reads=["r_t2", "neghalf"], writes=[res], tag="rrs12" + tag)

    POOL_SPLIT = KC

    def tt_bcast_split(out_t, out_res, in0_t, in0_res, tile_t, tile_res, op, tag):
        specs = []
        for eng, j0, j1 in (("vector", 0, POOL_SPLIT), ("gpsimd", POOL_SPLIT, KC)):
            if j1 <= j0:
                continue
            o3 = out_t[:].rearrange("p (j t) -> p j t", t=512)[:, j0:j1, :]
            i3 = in0_t[:].rearrange("p (j t) -> p j t", t=512)[:, j0:j1, :]
            b3 = tile_t[:].unsqueeze(1).broadcast_to((128, j1 - j0, 512))
            specs.append((eng,
                          (lambda e, o3=o3, i3=i3, b3=b3, op=op:
                           e.tensor_tensor(o3, i3, b3, op)),
                          [in0_res, tile_res], [out_res], f"{tag}.{eng[0]}"))
        p.par(specs)

    def tt_plain_split(out_t, out_res, in0_t, in0_res, in1_t, in1_res, op, tag):
        specs = []
        for eng, j0, j1 in (("vector", 0, POOL_SPLIT), ("gpsimd", POOL_SPLIT, KC)):
            if j1 <= j0:
                continue
            sl = slice(j0 * 512, j1 * 512)
            specs.append((eng,
                          (lambda e, sl=sl, op=op: e.tensor_tensor(
                              out_t[:, sl], in0_t[:, sl], in1_t[:, sl], op)),
                          [in0_res, in1_res], [out_res], f"{tag}.{eng[0]}"))
        p.par(specs)

    gw3 = gates_w.rearrange("(j p) f -> p j f", p=128)
    xT3 = xT.rearrange("n (j p) t -> p (n j) t", p=128)
    yT3 = yT.rearrange("n (j p) t -> p (n j) t", p=128)

    ctxA = contextlib.ExitStack()
    WA = alloc("WA", [128, KC * 2048], BF16, ctxA)
    lrw_bf = alloc("lrw_bf", [128, 8], BF16, ctxA)
    Sf = alloc("Sf", [128, 8192], F32, ctxA)
    Wf = alloc("Wf", [128, 8192], F32, ctxA)
    Sbf = alloc("Sbf", [128, 8192], BF16, ctxA)
    xa = alloc("xa", [128, KC * 512], BF16, ctxA)
    z2a = alloc("z2a", [128, KC * 512], BF16, ctxA)
    v_tm = alloc("v_tm", [128, CH * 1024], BF16, ctxA)
    k_tm = alloc("k_tm", [128, CH * 1024], BF16, ctxA)
    u_tm = alloc("u_tm", [128, CH * 1024], BF16, ctxA)
    ulr_tm = alloc("ulr_tm", [128, CH * 1024], BF16, ctxA)
    k_fm = alloc("k_fm", [128, CH * 1024], BF16, ctxA)
    _ksq1 = alloc("ksq_buf", [128, 1024], BF16, ctxA)
    ksq = [_ksq1, _ksq1]
    alpha_col = alloc("alpha_col", [128, 2 * CH], F32, ctxA)
    _csb1 = alloc("csb_buf", [128, 512], BF16, ctxA)
    csb = [_csb1, _csb1]
    lrm_col = alloc("lrm_col", [128, 1], F32, ctxA)
    Rbf = alloc("Rbf", [128, CH * 1024], BF16, ctxA)
    tbT = alloc("tbT", [128, CH * 128], BF16, ctxA)
    tbG = alloc("tbG", [128, 6 * 128], BF16, ctxA)
    tbL = alloc("tbL", [128, 3 * 128], BF16, ctxA)
    tbLT = alloc("tbLT", [128, 6 * 128], BF16, ctxA)
    tbS = alloc("tbS", [128, 128], BF16, ctxA)
    tbX = [alloc(f"tbX{i}", [128, 128], F32, ctxA) for i in range(2)]
    tbZ = [alloc(f"tbZ{i}", [128, 128], F32, ctxA) for i in range(2)]
    tbA = [alloc(f"tbA{i}", [128, 128], F32, ctxA) for i in range(2)]
    tbAb = [alloc(f"tbAb{i}", [128, 128], BF16, ctxA) for i in range(CH)]
    lrc = alloc("lrc", [128, CH], F32, ctxA)
    ksum = alloc("ksum", [128, CH], F32, ctxA)
    krs = alloc("krs", [128, CH], F32, ctxA)
    rowpA = alloc("rowpA", [1, 1536], BF16, ctxA)
    gbk_t = alloc("gbk_t", [128, 1024], BF16, ctxA)
    gbv_t = alloc("gbv_t", [128, 1024], BF16, ctxA)
    browkv = rowa[:][0:1, 0:2048]
    BROWKV_KEYS = ["r_m", "r_e2", "r_t", "r_rs"]
    browkv_bf = Rbf[:][0:1, 0:2048]

    p.op("gpsimd", lambda e: e.dma_start(
        out=WA[:].rearrange("p (j f) -> p j f", f=2048), in_=gw3[:, :, 1024:3072]),
        writes=["WA"], dma=True, tag="ld_WA")
    p.op("gpsimd", lambda e: e.dma_start(out=lrw_bf[:], in_=lrw_col),
         writes=["lrw_bf"], dma=True, tag="ld_lrw")
    p.op("sync", lambda e: e.dma_start(out=browkv, in_=gates_b_row[:, 1024:3072]),
         writes=BROWKV_KEYS, dma=True, tag="ld_gbrow")
    for i in range(4):
        pairs = [(dnb_bf[:, j:j + 1], WA[:, j * 2048 + i * 512: j * 2048 + (i + 1) * 512])
                 for j in range(KC)]
        mm_group("bT0", psT[0][0:1, 0:512], pairs, ["dnb_bf", "WA"], f"browkv{i}")
        p.op("vector", lambda e, i=i: e.tensor_tensor(
            browkv[0:1, i * 512:(i + 1) * 512], psT[0][0:1, 0:512],
            browkv[0:1, i * 512:(i + 1) * 512], ALU.add),
            reads=["bT0"] + BROWKV_KEYS, writes=BROWKV_KEYS, tag=f"browkv_add{i}")
    p.op("vector", lambda e: e.tensor_copy(browkv_bf, browkv),
         reads=BROWKV_KEYS, writes=["Rbf"])
    for i in range(2):
        bcast(browkv_bf[0:1, i * 512:(i + 1) * 512], "Rbf",
              gbk_t[:, i * 512:(i + 1) * 512], "gbk_t", "bT0", psT[0], f"gbk{i}")
    for i in range(2):
        bcast(browkv_bf[0:1, 1024 + i * 512: 1024 + (i + 1) * 512], "Rbf",
              gbv_t[:, i * 512:(i + 1) * 512], "gbv_t", "bT0", psT[0], f"gbv{i}")
    for j in range(KC):
        p.op("vector", lambda e, j=j: e.tensor_scalar_mul(
            WA[:, j * 2048:(j + 1) * 2048], WA[:, j * 2048:(j + 1) * 2048],
            sb["dn_g"][:, j:j + 1]),
            reads=["WA", "dn_g"], writes=["WA"], tag=f"foldWA{j}")
    p.op("vector", lambda e: e.tensor_tensor(lrw_bf[:], lrw_bf[:], sb["dn_g"][:], ALU.mult),
         reads=["lrw_bf", "dn_g"], writes=["lrw_bf"], tag="fold_lr")
    for i in range(4):
        pairs = [(ones_col[:], WA[:, j * 2048 + i * 512: j * 2048 + (i + 1) * 512])
                 for j in range(KC)]
        mm_group("bT0", psT[0][0:1, 0:512], pairs, ["ones_col", "WA"], f"wacs{i}")
        p.op("vector", lambda e, i=i: e.tensor_scalar_mul(
            rowa[:][0:1, i * 512:(i + 1) * 512], psT[0][0:1, 0:512], 1.0 / C),
            reads=["bT0"], writes=[BROWKV_KEYS[i]], tag=f"wacse{i}")
    for i in range(4):
        p.op("vector", lambda e, i=i: e.tensor_copy(
            csb[i % 2][:][0:1, 0:512], rowa[:][0:1, i * 512:(i + 1) * 512]),
            reads=[BROWKV_KEYS[i]], writes=["csb0"], tag=f"wacsb{i}")
        p.op("tensor", lambda e, i=i: e.matmul(
            psT[0][:, 0:512], ones_row[:], csb[i % 2][:][0:1, 0:512],
            start=True, stop=True, skip_group_check=True),
            reads=["ones_row", "csb0"], writes=["bT0"], tag=f"wacb{i}")
        p.op("scalar", lambda e, i=i: e.copy(csb[i % 2][:], psT[0][:, 0:512]),
             reads=["bT0"], writes=["csb0"], tag=f"wacbe{i}")
        for j in range(KC):
            p.op("vector", lambda e, i=i, j=j: e.tensor_tensor(
                WA[:, j * 2048 + i * 512: j * 2048 + (i + 1) * 512],
                WA[:, j * 2048 + i * 512: j * 2048 + (i + 1) * 512],
                csb[i % 2][:], ALU.subtract),
                reads=["WA", "csb0"], writes=["WA"], tag=f"wac{i}.{j}")
    pairs = [(ones_col[:], lrw_bf[:, j:j + 1]) for j in range(KC)]
    mm_group("bT1", psT[1][0:1, 0:1], pairs, ["ones_col", "lrw_bf"], "lrcs")
    p.op("vector", lambda e: e.tensor_scalar_mul(
        rowa[:][0:1, 0:1], psT[1][0:1, 0:1], 1.0 / C),
        reads=["bT1"], writes=["r_m"], tag="lrcse")
    p.op("vector", lambda e: e.tensor_copy(csb[0][:][0:1, 0:1], rowa[:][0:1, 0:1]),
         reads=["r_m", "csb0"], writes=["csb0"], tag="lrcsb")
    p.op("tensor", lambda e: e.matmul(psT[1][:, 0:1], ones_row[:],
                                      csb[0][:][0:1, 0:1],
                                      start=True, stop=True, skip_group_check=True),
         reads=["ones_row", "csb0"], writes=["bT1"], tag="lrcb")
    p.op("scalar", lambda e: e.copy(lrm_col[:], psT[1][:, 0:1]),
         reads=["bT1"], writes=["lrm_col"], tag="lrcbe")
    p.op("vector", lambda e: e.tensor_scalar(
        lrw_bf[:], lrw_bf[:], lrm_col[:], None, ALU.subtract),
        reads=["lrw_bf", "lrm_col"], writes=["lrw_bf"], tag="lrc_sub")
    dlr = alloc("dlr", [128, 8], BF16, ctxA)
    p.op("vector", lambda e: e.tensor_tensor(dlr[:], sb["dn_b"][:], sb["lrw_f"][:], ALU.mult),
         reads=["dn_b", "lrw_f"], writes=["dlr"], tag="dlr")
    pairs = [(dlr[:, j:j + 1], ones_col[:]) for j in range(KC)]
    mm_group("bT1", psT[1][0:1, 0:1], pairs, ["dlr", "ones_col"], "lrb_eff")
    p.op("vector", lambda e: e.tensor_tensor(rowa[:][0:1, 0:1], psT[1][0:1, 0:1],
                                             lrb_sb[:], ALU.add),
         reads=["bT1", "lrb"], writes=["r_m"], tag="lrb_eff_add")
    lrb_bf = alloc("lrb_bf", [1, 1], BF16, ctxA)
    p.op("vector", lambda e: e.tensor_copy(lrb_bf[:], rowa[:][0:1, 0:1]),
         reads=["r_m"], writes=["lrb_bf"])
    p.op("tensor", lambda e: e.matmul(psT[1][:, 0:1], ones_row[:], lrb_bf[:],
                                      start=True, stop=True, skip_group_check=True),
         reads=["ones_row", "lrb_bf"], writes=["bT1"], tag="lrb_bc")
    p.op("scalar", lambda e: e.copy(lrb_col[:], psT[1][:, 0:1]),
         reads=["bT1"], writes=["lrb_col"], tag="lrb_col")

    AU = [(n, s) for n in range(NSEQ) for s in range(NST)]

    def a_load(u):
        n, s = u
        p.op("gpsimd", lambda e, n=n, s=s: e.dma_start(
            out=xa[:].rearrange("p (j t) -> p j t", t=512),
            in_=xT3[:, n * KC:(n + 1) * KC, s * ST:s * ST + 512]),
            writes=["xa"], dma=True, tag=f"ldxA{n}.{s}")

    def a_stats(u):
        n, s = u
        tg = f"A{n}s{s}"
        up = (n * NST + s) % 2
        p.op("scalar", lambda e: e.activation(z2a[:], xa[:], AF.Square),
             reads=["xa"], writes=["z2a"], tag="sq" + tg)
        mm_group("bT0", psT[0][0:1, 0:512],
                 [(ones_col[:], xa[:, j * 512:(j + 1) * 512]) for j in range(KC)],
                 ["ones_col", "xa"], "ssx" + tg)
        mm_group("bT1", psT[1][0:1, 0:512],
                 [(ones_col[:], z2a[:, j * 512:(j + 1) * 512]) for j in range(KC)],
                 ["ones_col", "z2a"], "ssq" + tg)
        ln_stat_rows("bT0", "bT1", psT[0], psT[1],
                     m_out=(rowpA[0:1, 0:512], "rowpA"),
                     rs_out=(rowpA[0:1, 512:1024], "rowpA"),
                     rs12_out=(rowpA[0:1, 1024:1536], "rowpA"), tag="a" + tg)
        p.op("sync", lambda e, n=n, s=s: e.dma_start(
            out=rows_hbm[0:1, (n * NST + s) * 2048:(n * NST + s) * 2048 + 1536],
            in_=rowpA[:]), reads=["rowpA"], dma=True, tag="strow" + tg)

    def emit_ktr(sub, tg):
        pbf = psT[0][:].bitcast(BF16)

        def trf(e, sub=sub, pbf=pbf):
            ins = None
            for j in range(KC):
                ins = e.transpose(
                    pbf[:, j * 128:(j + 1) * 128],
                    k_tm[:, sub * 1024 + j * 128: sub * 1024 + (j + 1) * 128],
                    ident_bf[:])
            return ins
        p.op("tensor", trf, reads=["k_tm", "ident"], writes=["bT0"],
             tag=f"ktr{sub}" + tg)
        p.op("scalar", lambda e, sub=sub, pbf=pbf: e.copy(
            k_fm[:, sub * 1024:(sub + 1) * 1024], pbf[:]),
            reads=["bT0"], writes=["k_fm"], tag=f"ktre{sub}" + tg)

    def a_proj(u):
        n, s = u
        tg = f"A{n}s{s}"
        up = (n * NST + s) % 2
        ares = f"alpha{up}"
        for sub in range(CH):
            acol = alpha_col[:, up * CH + sub: up * CH + sub + 1]
            for h in range(2):
                pb, pr = ROT4[(sub * 2 + h) % 3]
                pairs = [(xa[:, j * 512 + sub * 128: j * 512 + (sub + 1) * 128],
                          WA[:, j * 2048 + 1024 + h * 512: j * 2048 + 1024 + (h + 1) * 512])
                         for j in range(KC)]
                mm_group(pr, pb[:], pairs, ["xa", "WA"], f"vmm{sub}.{h}" + tg)
                if sub == 0 and h == 0:
                    pbf1 = psT[1][:].bitcast(BF16)

                    def tra(e, pbf1=pbf1):
                        ins = None
                        for c in range(CH):
                            ins = e.transpose(
                                pbf1[:, 2 * c:2 * c + 1],
                                rowpA[0:1, 1024 + c * 128:1024 + (c + 1) * 128],
                                ones_row[0:1, 0:1])
                        return ins
                    p.op("tensor", tra, reads=["rowpA", "ones_row"],
                         writes=["bT1"], tag="atr" + tg)
                    src = pbf1[:, 0:2 * CH].rearrange(
                        "p (c r) -> p c r", r=2)[:, :, 0:1]
                    p.op("vector", lambda e, up=up, src=src: e.tensor_copy(
                        alpha_col[:, up * CH:(up + 1) * CH], src),
                        reads=["bT1"], writes=[f"alpha{up}"], tag="acol" + tg)
                p.op("vector", lambda e, sub=sub, h=h, pb=pb, acol=acol:
                     e.scalar_tensor_tensor(
                         v_tm[:, sub * 1024 + h * 512: sub * 1024 + (h + 1) * 512],
                         pb[:], acol, gbv_t[:, h * 512:(h + 1) * 512],
                         ALU.mult, ALU.add),
                     reads=[pr, ares, "gbv_t"], writes=["v_tm"],
                     tag=f"vev{sub}.{h}" + tg)
            for h in range(2):
                pb, pr = ROT4[(sub * 2 + h + 2) % 3]
                pairs = [(xa[:, j * 512 + sub * 128: j * 512 + (sub + 1) * 128],
                          WA[:, j * 2048 + h * 512: j * 2048 + (h + 1) * 512])
                         for j in range(KC)]
                mm_group(pr, pb[:], pairs, ["xa", "WA"], f"kmm{sub}.{h}" + tg)
                p.op("vector", lambda e, sub=sub, h=h, pb=pb, acol=acol:
                     e.scalar_tensor_tensor(
                         k_tm[:, sub * 1024 + h * 512: sub * 1024 + (h + 1) * 512],
                         pb[:], acol, gbk_t[:, h * 512:(h + 1) * 512],
                         ALU.mult, ALU.add),
                     reads=[pr, ares, "gbk_t"], writes=["k_tm"],
                     tag=f"kev{sub}.{h}" + tg)
            pairs = [(xa[:, j * 512 + sub * 128: j * 512 + (sub + 1) * 128],
                      lrw_bf[:, j:j + 1]) for j in range(KC)]
            mm_group("bS0", psS[0][:, 0:1], pairs, ["xa", "lrw_bf"], f"lr{sub}" + tg)
            p.op("vector", lambda e, sub=sub, acol=acol: e.tensor_scalar(
                lrc[:, sub:sub + 1], psS[0][:, 0:1], acol, lrb_col[:],
                ALU.mult, ALU.add),
                reads=["bS0", ares, "lrb_col"], writes=["lrc"], tag=f"lrev{sub}" + tg)
            kp = sub % 2
            p.op("scalar", lambda e, sub=sub, kp=kp: e.activation(
                ksq[kp][:], k_tm[:, sub * 1024:(sub + 1) * 1024], AF.Square,
                accum_out=ksum[:, sub:sub + 1]),
                reads=["k_tm"], writes=[f"ksq{kp}", "ksum"], tag=f"ksq{sub}" + tg)
            p.op("scalar", lambda e, sub=sub: e.activation(
                krs[:, sub:sub + 1], ksum[:, sub:sub + 1], AF.Sqrt),
                reads=["ksum"], writes=["krs"], tag=f"ksd{sub}" + tg)
            p.op("vector", lambda e, sub=sub: e.reciprocal(
                krs[:, sub:sub + 1], krs[:, sub:sub + 1]),
                reads=["krs"], writes=["krs"], tag=f"krec{sub}" + tg)
            p.op("vector", lambda e, sub=sub: e.tensor_scalar_mul(
                k_tm[:, sub * 1024:(sub + 1) * 1024],
                k_tm[:, sub * 1024:(sub + 1) * 1024], krs[:, sub:sub + 1]),
                reads=["k_tm", "krs"], writes=["k_tm"], tag=f"knrm{sub}" + tg)
            if sub >= 1:
                emit_ktr(sub - 1, tg)

    def a_tbuild_levels(u, cc):
        n, s = u
        tg = f"A{n}s{s}"
        par = cc % 2
        bx, bz, ba, bab = tbX[par], tbZ[par], tbA[par], tbAb[cc]
        rx, rz, ra, rab = f"tbX{par}", f"tbZ{par}", f"tbA{par}", f"tbAb{cc}"
        pT0, pT1 = (psT[0], psT[1]) if par == 0 else (psS[0], psS[1])
        rT0, rT1 = ("bT0", "bT1") if par == 0 else ("bS0", "bS1")

        def lev0():
            pairs = [(k_fm[:, cc * 1024 + j * 128: cc * 1024 + (j + 1) * 128],
                      k_fm[:, cc * 1024 + j * 128: cc * 1024 + (j + 1) * 128])
                     for j in range(KC)]
            mm_group(rT0, pT0[:, 0:128], pairs, ["k_fm"], f"kk{cc}" + tg)
            p.op("vector", lambda e: e.tensor_tensor(
                bx[:], pT0[:, 0:128], maskU[:], ALU.mult),
                reads=[rT0, "maskU"], writes=[rx], tag=f"tX0{cc}" + tg)
            p.op("vector", lambda e: e.tensor_tensor(
                bz[:], pT0[:, 0:128], maskL[:], ALU.mult),
                reads=[rT0, "maskL"], writes=[rz], tag=f"tZ0{cc}" + tg)
            p.op("vector", lambda e: e.tensor_tensor(
                ba[:], ident_bf[:], bx[:], ALU.subtract),
                reads=["ident", rx], writes=[ra], tag=f"tA0{cc}" + tg)

        ev_eng = "scalar" if par == 0 else "vector"

        def mklev(jj):
            def lev():
                if jj < 6:
                    mm_group(rT1, pT1[:, 0:128], [(bz[:], bx[:])], [rz, rx],
                             f"tXn{cc}.{jj}" + tg)
                mm_group(rT0, pT0[:, 128:256], [(bx[:], bz[:])], [rx, rz],
                         f"tZn{cc}.{jj}" + tg)
                cpf = (lambda e, o, i: e.copy(o, i)) if ev_eng == "scalar" \
                    else (lambda e, o, i: e.tensor_copy(o, i))
                if jj < 6:
                    p.op(ev_eng, lambda e, cpf=cpf: cpf(e, bx[:], pT1[:, 0:128]),
                         reads=[rT1], writes=[rx], tag=f"tXe{cc}.{jj}" + tg)
                p.op(ev_eng, lambda e, cpf=cpf: cpf(e, bz[:], pT0[:, 128:256]),
                     reads=[rT0], writes=[rz], tag=f"tZe{cc}.{jj}" + tg)
                mm_group(rT1, pT1[:, 128:256], [(bz[:], ba[:])], [rz, ra],
                         f"tAm{cc}.{jj}" + tg)
                p.op("vector", lambda e: e.tensor_tensor(
                    ba[:], pT1[:, 128:256], ba[:], ALU.add),
                    reads=[rT1, ra], writes=[ra], tag=f"tAe{cc}.{jj}" + tg)
            return lev

        def levf():
            p.op("vector", lambda e: e.tensor_copy(bab[:], ba[:]),
                 reads=[ra], writes=[rab], tag=f"tAc{cc}" + tg)
        return [lev0] + [mklev(jj) for jj in range(1, 7)] + [levf]

    def a_tbuild_pair(u, cc0, cc1=None, post_lev0=None):
        la = a_tbuild_levels(u, cc0)
        lb = a_tbuild_levels(u, cc1) if cc1 is not None else []
        for i in range(max(len(la), len(lb))):
            if i < len(la):
                la[i]()
            if i < len(lb):
                lb[i]()
            if i == 0 and post_lev0 is not None:
                post_lev0()

    PIDX = {(0, 1): 0, (0, 2): 1, (1, 2): 2, (0, 3): 3, (1, 3): 4, (2, 3): 5}
    LIDX = {(1, 0): 0, (2, 0): 1, (2, 1): 2}

    def a_chunks(u):
        n, s = u
        tg = f"A{n}s{s}"
        has_p = s > 0

        def mk_pfill(cc, h):
            def f():
                pairs = [(k_fm[:, cc * 1024 + j * 128: cc * 1024 + (j + 1) * 128],
                          Sbf[:, j * 1024 + h * 512: j * 1024 + (h + 1) * 512])
                         for j in range(KC)]
                pbp, prp = (psA[h], f"bA{h}")
                mm_group(prp, pbp[:], pairs, ["k_fm", "Sbf"], f"pks{cc}.{h}" + tg)
                p.op("vector", lambda e, cc=cc, h=h, pbp=pbp: e.tensor_tensor(
                    Rbf[:, cc * 1024 + h * 512: cc * 1024 + (h + 1) * 512],
                    v_tm[:, cc * 1024 + h * 512: cc * 1024 + (h + 1) * 512],
                    pbp[:], ALU.subtract),
                    reads=["v_tm", prp], writes=["Rbf"], tag=f"R{cc}.{h}" + tg)
            return f

        def mk_gfill(m, c):
            def f():
                pi = PIDX[(m, c)]
                pairs = [(k_fm[:, m * 1024 + j * 128: m * 1024 + (j + 1) * 128],
                          k_fm[:, c * 1024 + j * 128: c * 1024 + (j + 1) * 128])
                         for j in range(KC)]
                mm_group("bB0", psB[0][:, 0:128], pairs, ["k_fm"],
                         f"gx{m}.{c}" + tg)
                p.op("scalar", lambda e, pi=pi: e.copy(
                    tbG[:, pi * 128:(pi + 1) * 128], psB[0][:, 0:128]),
                    reads=["bB0"], writes=["tbG"], tag=f"gxe{m}.{c}" + tg)
            return f

        fillers = [mk_gfill(m, c) for (m, c) in PIDX]
        if has_p:
            fillers += [mk_pfill(cc, h) for cc in range(CH) for h in range(2)]

        la = a_tbuild_levels(u, 0)
        lb = a_tbuild_levels(u, 1)
        lc = a_tbuild_levels(u, 2)
        ld = a_tbuild_levels(u, 3)
        fi = 0

        def fill(k):
            nonlocal fi
            for _ in range(k):
                if fi < len(fillers):
                    fillers[fi]()
                    fi += 1
        for i in range(8):
            la[i]()
            lb[i]()
            if i == 0:
                emit_ktr(3, tg)
            fill(1)
        for i in range(8):
            lc[i]()
            ld[i]()
            fill(1)
        fill(len(fillers))

        pbf0 = psT[0][:].bitcast(BF16)
        for c in range(CH):
            p.op("tensor", lambda e, c=c, pbf0=pbf0: e.transpose(
                pbf0[:, 0:128], tbAb[c][:], ident_bf[:]),
                reads=[f"tbAb{c}", "ident"], writes=["bT0"], tag=f"ttr{c}" + tg)
            p.op("scalar", lambda e, c=c, pbf0=pbf0: e.copy(
                tbT[:, c * 128:(c + 1) * 128], pbf0[:, 0:128]),
                reads=["bT0"], writes=["tbT"], tag=f"ttre{c}" + tg)

        def l_block(c, j):
            pi = PIDX[(j, c)]
            pairs = []
            for m in range(j, c):
                if m == j:
                    rhs = tbT[:, j * 128:(j + 1) * 128]
                    rres = "tbT"
                else:
                    rhs = tbL[:, LIDX[(m, j)] * 128:(LIDX[(m, j)] + 1) * 128]
                    rres = "tbL"
                pairs.append(((tbG[:, PIDX[(m, c)] * 128:(PIDX[(m, c)] + 1) * 128],
                               rhs), rres))
            mm_group("bB0", psB[0][:, 0:128], [pq for pq, _ in pairs],
                     ["tbG"] + [rr for _, rr in pairs], f"ls{c}.{j}" + tg)
            p.op("scalar", lambda e: e.copy(tbS[:], psB[0][:, 0:128]),
                 reads=["bB0"], writes=["tbS"], tag=f"lse{c}.{j}" + tg)
            mm_group("bB0", psB[0][:, 128:256],
                     [(tbAb[c][:], tbS[:])],
                     [f"tbAb{c}", "tbS"], f"lt{c}.{j}" + tg)
            if (c, j) in LIDX:
                li = LIDX[(c, j)]
                ldst, lres = tbL[:, li * 128:(li + 1) * 128], "tbL"
            else:
                ldst, lres = tbS[:], "tbS"
            p.op("vector", lambda e, ldst=ldst: e.tensor_scalar_mul(
                ldst, psB[0][:, 128:256], -1.0),
                reads=["bB0"], writes=[lres], tag=f"lte{c}.{j}" + tg)
            p.op("tensor", lambda e, ldst=ldst, pbf0=pbf0: e.transpose(
                pbf0[:, 0:128], ldst, ident_bf[:]),
                reads=[lres, "ident"], writes=["bT0"], tag=f"ltr{c}.{j}" + tg)
            p.op("scalar", lambda e, pi=pi, pbf0=pbf0: e.copy(
                tbLT[:, pi * 128:(pi + 1) * 128], pbf0[:, 0:128]),
                reads=["bT0"], writes=["tbLT"], tag=f"ltre{c}.{j}" + tg)

        def u_solve(cc):
            r_ap, r_res = ((Rbf[:], "Rbf") if has_p
                           else (v_tm[:], "v_tm"))
            for h in range(2):
                pairs = [(tbAb[cc][:],
                          r_ap[:, cc * 1024 + h * 512: cc * 1024 + (h + 1) * 512])]
                reads = [f"tbAb{cc}", r_res]
                for j in range(cc):
                    pi = PIDX[(j, cc)]
                    pairs.append((tbLT[:, pi * 128:(pi + 1) * 128],
                                  r_ap[:, j * 1024 + h * 512: j * 1024 + (h + 1) * 512]))
                if cc > 0:
                    reads.append("tbLT")
                mm_group(f"bA{h}", psA[h][:], pairs, reads, f"u{cc}.{h}" + tg)
                p.op("vector", lambda e, cc=cc, h=h: e.tensor_copy(
                    u_tm[:, cc * 1024 + h * 512: cc * 1024 + (h + 1) * 512],
                    psA[h][:]),
                    reads=[f"bA{h}"], writes=["u_tm"], tag=f"ue{cc}.{h}" + tg)
            p.op("scalar", lambda e, cc=cc: e.activation(
                ulr_tm[:, cc * 1024:(cc + 1) * 1024],
                u_tm[:, cc * 1024:(cc + 1) * 1024], AF.Identity,
                scale=lrc[:, cc:cc + 1]),
                reads=["u_tm", "lrc"], writes=["ulr_tm"], tag=f"ulr{cc}" + tg)

        l_block(1, 0)
        l_block(2, 1)
        l_block(3, 2)
        u_solve(0)
        l_block(2, 0)
        l_block(3, 1)
        u_solve(1)
        l_block(3, 0)
        u_solve(2)
        u_solve(3)

    def a_state(u):
        n, s = u
        tg = f"A{n}s{s}"
        targets = [] if s == NST - 1 else [("Sf", u_tm, "u_tm", Sf)]
        targets.append(("Wfr", ulr_tm, "ulr_tm", Wf))
        for res_nm, rhs_t, rhs_res, tgt in targets:
            for mj in range(KC):
                for h in range(2):
                    pb, pr = ROTS[(mj * 2 + h) % 4]
                    pairs = [(k_tm[:, cc * 1024 + mj * 128: cc * 1024 + (mj + 1) * 128],
                              rhs_t[:, cc * 1024 + h * 512: cc * 1024 + (h + 1) * 512])
                             for cc in range(CH)]
                    mm_group(pr, pb[:], pairs,
                             ["k_tm", rhs_res], f"up{res_nm}{mj}.{h}" + tg)
                    dst = tgt[:, mj * 1024 + h * 512: mj * 1024 + (h + 1) * 512]
                    if s == 0:
                        p.op("vector", lambda e, dst=dst, pb=pb: e.tensor_copy(
                            dst, pb[:]),
                            reads=[pr], writes=[res_nm],
                            tag=f"upc{res_nm}{mj}.{h}" + tg)
                    else:
                        p.op("vector", lambda e, dst=dst, pb=pb: e.tensor_tensor(
                            dst, pb[:], dst, ALU.add),
                            reads=[pr, res_nm], writes=[res_nm],
                            tag=f"upa{res_nm}{mj}.{h}" + tg)
        if s < NST - 1:
            p.op("vector", lambda e: e.tensor_copy(Sbf[:], Sf[:]),
                 reads=["Sf"], writes=["Sbf"], tag="sbf" + tg)

    def a_wstash(n):
        sfx = f"A{n}"
        p.op("vector", lambda e: e.tensor_copy(Sbf[:], Wf[:]),
             reads=["Wfr"], writes=["Sbf"], tag=f"wbf{sfx}")
        p.op("sync", lambda e, n=n: e.dma_start(
            out=Wstash[:, n * 8192:(n + 1) * 8192], in_=Sbf[:]),
            reads=["Sbf"], writes=[f"Wst{n}"], dma=True, tag=f"wst{sfx}")
        if debug and n == 0:
            for nm, t, rs in (("k_tm", k_tm, "k_tm"), ("v_tm", v_tm, "v_tm"),
                              ("u_tm", u_tm, "u_tm"), ("lrc", lrc, "lrc"),
                              ("Wf", Wf, "Wfr")):
                tp = tap(nm, [128, t[:].shape[1]], F32 if nm in ("lrc", "Wf") else BF16)
                p.op("sync", lambda e, tp=tp, t=t: e.dma_start(out=tp, in_=t[:]),
                     reads=[rs], dma=True, tag="tap_" + nm)

    a_load(AU[0])
    a_stats(AU[0])
    for i, u in enumerate(AU):
        a_proj(u)
        a_chunks(u)
        if i + 1 < len(AU):
            a_load(AU[i + 1])
            a_stats(AU[i + 1])
        a_state(u)
        if u[1] == NST - 1:
            a_wstash(u[0])
    ctxA.close()
    p.barrier()

    ctxB = contextlib.ExitStack()
    Wq = alloc("Wq", [128, KC * 1024], BF16, ctxB)
    Wo = alloc("Wo", [128, KC * 1024], BF16, ctxB)
    W1 = alloc("W1", [128, KC * 1024], BF16, ctxB)
    W2 = alloc("W2", [128, KC * 1024], BF16, ctxB)
    Wcur = alloc("Wcur", [128, 8192], BF16, ctxB)
    xb0 = alloc("xb0", [128, KC * 512], F32, ctxB)
    xb1 = alloc("xb1", [128, KC * 512], F32, ctxB)
    xcb = alloc("xcb", [128, KC * 512], BF16, ctxB)
    qfb = alloc("qfb", [128, KC * 512], BF16, ctxB)
    yf = alloc("yf", [128, KC * 512], BF16, ctxB)
    cf = alloc("cf", [128, KC * 512], F32, ctxB)
    sqb = alloc("sqb", [128, KC * 512], BF16, ctxB)
    t_al = [alloc(f"t_al{pp}", [128, 512], BF16, ctxB) for pp in range(2)]
    t_rs1 = alloc("t_rs1", [128, 512], BF16, ctxB)
    t_mrs1 = alloc("t_mrs1", [128, 512], BF16, ctxB)
    t_rs3 = alloc("t_rs3", [128, 512], BF16, ctxB)
    t_rs4 = alloc("t_rs4", [128, 512], BF16, ctxB)
    t_mrs4 = alloc("t_mrs4", [128, 512], BF16, ctxB)
    z3b = alloc("z3b", [128, KC * 512], BF16, ctxB)
    rowpB = [alloc(f"rowpB{i}", [1, 1536], BF16, ctxB) for i in range(2)]
    rowbf = alloc("rowbf", [1, 1024], BF16, ctxB)
    biasq_bf = alloc("biasq_bf", [128, 8], BF16, ctxB)
    ybias_col = alloc("ybias_col", [128, 8], F32, ctxB)
    browq = rowa[:][0:1, 0:1024]
    BROWQ_KEYS = ["r_m", "r_e2"]

    p.op("gpsimd", lambda e: e.dma_start(
        out=Wq[:].rearrange("p (j f) -> p j f", f=1024), in_=gw3[:, :, 0:1024]),
        writes=["Wq"], dma=True, tag="ld_Wq")
    for qeng, (nm, t, src) in zip(("gpsimd", "gpsimd", "gpsimd"),
                                  (("Wo", Wo, out_w), ("W1", W1, w1),
                                   ("W2", W2, w2))):
        r3 = src.rearrange("(j p) f -> p j f", p=128)
        p.op(qeng, lambda e, t=t, r3=r3: e.dma_start(
            out=t[:].rearrange("p (j f) -> p j f", f=1024), in_=r3),
            writes=[nm], dma=True, tag="ld_" + nm)
    for rown, wt, wres, bvec, bres in (("browq_q", Wq, "Wq", dnb_bf, "dnb_bf"),
                                       ("browq_1", W1, "W1", mnb_bf, "mnb_bf")):
        for i in range(2):
            pairs = [(bvec[:, j:j + 1], wt[:, j * 1024 + i * 512: j * 1024 + (i + 1) * 512])
                     for j in range(KC)]
            mm_group("bT0", psT[0][0:1, 0:512], pairs, [bres, wres], f"{rown}{i}")
            p.op("vector", lambda e, i=i: e.tensor_copy(
                browq[0:1, i * 512:(i + 1) * 512], psT[0][0:1, 0:512]),
                reads=["bT0"], writes=BROWQ_KEYS, tag=f"{rown}e{i}")
        tcol = biasq_col if rown == "browq_q" else bias1_col
        tcol_res = "biasq_col" if rown == "browq_q" else "bias1_col"
        hostb = sb["gb_col"][:, 0:8] if rown == "browq_q" else sb["b1a"][:]
        hostb_res = "gb_col" if rown == "browq_q" else "b1a"

        def trrow(e, browq=browq):
            ins = None
            for j in range(KC):
                ins = e.transpose(psT[1][:, j:j + 1], browq[0:1, j * 128:(j + 1) * 128],
                                  one1_f[:])
            return ins
        p.op("tensor", trrow, reads=BROWQ_KEYS + ["one1"], writes=["bT1"], tag=rown + "tr")
        p.op("vector", lambda e, tcol=tcol, hostb=hostb: e.tensor_tensor(
            tcol[:], psT[1][:, 0:8], hostb, ALU.add),
            reads=["bT1", hostb_res], writes=[tcol_res], tag=rown + "col")
    p.op("vector", lambda e: e.tensor_copy(biasq_bf[:], biasq_col[:]),
         reads=["biasq_col"], writes=["biasq_bf"], tag="biasq_bf")
    for j in range(KC):
        p.op("vector", lambda e, j=j: e.tensor_scalar_mul(
            Wq[:, j * 1024:(j + 1) * 1024], Wq[:, j * 1024:(j + 1) * 1024],
            sb["dn_g"][:, j:j + 1]),
            reads=["Wq", "dn_g"], writes=["Wq"], tag=f"foldWq{j}")
        p.op("vector", lambda e, j=j: e.tensor_scalar_mul(
            W1[:, j * 1024:(j + 1) * 1024], W1[:, j * 1024:(j + 1) * 1024],
            sb["mn_g"][:, j:j + 1]),
            reads=["W1", "mn_g"], writes=["W1"], tag=f"foldW1{j}")
    for wt, wres in ((Wq, "Wq"), (W1, "W1")):
        for i in range(2):
            pairs = [(ones_col[:], wt[:, j * 1024 + i * 512: j * 1024 + (i + 1) * 512])
                     for j in range(KC)]
            mm_group("bT0", psT[0][0:1, 0:512], pairs, ["ones_col", wres],
                     f"{wres}cs{i}")
            p.op("vector", lambda e, i=i: e.tensor_scalar_mul(
                browq[0:1, i * 512:(i + 1) * 512], psT[0][0:1, 0:512], 1.0 / C),
                reads=["bT0"], writes=BROWQ_KEYS, tag=f"{wres}cse{i}")
        p.op("vector", lambda e: e.tensor_copy(rowbf[:], browq),
             reads=BROWQ_KEYS, writes=["rowbf"], tag=f"{wres}csbf")
        prep_tiles = [(t_rs3, "t_rs3"), (t_rs4, "t_rs4")]
        for i in range(2):
            bcast(rowbf[0:1, i * 512:(i + 1) * 512], "rowbf", prep_tiles[i][0][:],
                  prep_tiles[i][1], "bT0", psT[0], f"{wres}cb{i}")
        for j in range(KC):
            for i in range(2):
                p.op("vector", lambda e, j=j, i=i, wt=wt: e.tensor_tensor(
                    wt[:, j * 1024 + i * 512: j * 1024 + (i + 1) * 512],
                    wt[:, j * 1024 + i * 512: j * 1024 + (i + 1) * 512],
                    prep_tiles[i][0][:], ALU.subtract),
                    reads=[wres, prep_tiles[i][1]], writes=[wres],
                    tag=f"{wres}c{j}.{i}")

    B1U = [(n, s) for n in range(NSEQ) for s in range(NST)]

    def b1_ldrow(u):
        n, s = u
        up = (n * NST + s) % 2
        p.op("sync", lambda e, n=n, s=s, up=up: e.dma_start(
            out=rowpB[up][:],
            in_=rows_hbm[0:1, (n * NST + s) * 2048:(n * NST + s) * 2048 + 1536]),
            writes=[f"rowpB{up}"], dma=True, tag=f"ldrowB1n{n}s{s}")

    def b1_load(u):
        n, s = u
        xb = (xb0, xb1)[s % 2]
        p.op("sync", lambda e, n=n, s=s, xb=xb: e.dma_start(
            out=xb[:].rearrange("p (j t) -> p j t", t=512),
            in_=xT3[:, n * KC:(n + 1) * KC, s * ST:s * ST + 512]),
            writes=[f"xb{s % 2}"], dma=True, tag=f"ldxB1n{n}s{s}")
        p.op("gpsimd", lambda e, n=n, s=s: e.dma_start(
            out=xcb[:].rearrange("p (j t) -> p j t", t=512),
            in_=xT3[:, n * KC:(n + 1) * KC, s * ST:s * ST + 512]),
            writes=["xcb"], dma=True, tag=f"ldxcB1n{n}s{s}")

    def b1_q(u):
        n, s = u
        tg = f"B1n{n}s{s}"
        par = (n * NST + s) % 2
        bcast(rowpB[par][0:1, 1024:1536], f"rowpB{par}", t_al[par][:],
              f"t_al{par}", "bT0", psT[0], "al" + tg)
        for m in range(KC):
            pb, pr = ROT4[m % 3]
            pairs = [(Wq[:, j * 1024 + m * 128: j * 1024 + (m + 1) * 128],
                      xcb[:, j * 512:(j + 1) * 512]) for j in range(KC)]
            mm_group(pr, pb[:], pairs, ["Wq", "xcb"], f"q{m}" + tg)
            p.op("vector", lambda e, m=m, pb=pb, par=par: e.tensor_tensor(
                qfb[:, m * 512:(m + 1) * 512], pb[:], t_al[par][:],
                ALU.mult),
                reads=[pr, f"t_al{par}"], writes=["qfb"], tag=f"qe{m}" + tg)
        if s == 0:
            p.op("sync", lambda e, n=n: e.dma_start(
                out=Wcur[:], in_=Wstash[:, n * 8192:(n + 1) * 8192]),
                reads=[f"Wst{n}"], writes=["Wcur"], dma=True, tag=f"ldWcur{n}")
            for m in range(KC):
                pairs = [(Wcur[:, j * 1024 + m * 128: j * 1024 + (m + 1) * 128],
                          biasq_bf[:, j:j + 1]) for j in range(KC)]
                mm_group("bT1", psT[1][:, m:m + 1], pairs,
                         ["Wcur", "biasq_bf"], f"yb{m}n{n}")
            p.op("vector", lambda e: e.tensor_copy(ybias_col[:], psT[1][:, 0:8]),
                 reads=["bT1"], writes=["ybias_col"], tag=f"ybe{n}")

    def b1_y(u):
        n, s = u
        tg = f"B1n{n}s{s}"
        par = (n * NST + s) % 2
        for m in range(KC):
            pb, pr = ROT4[m % 3]
            pairs = [(Wcur[:, j * 1024 + m * 128: j * 1024 + (m + 1) * 128],
                      qfb[:, j * 512:(j + 1) * 512]) for j in range(KC)]
            mm_group(pr, pb[:], pairs, ["Wcur", "qfb"], f"y{m}" + tg)
            p.op("scalar", lambda e, m=m, pb=pb: e.activation(
                yf[:, m * 512:(m + 1) * 512], pb[:], AF.Identity,
                bias=ybias_col[:, m:m + 1]),
                reads=[pr, "ybias_col"], writes=["yf"], tag=f"ye{m}" + tg)

    def b1_o(u, nxt):
        n, s = u
        tg = f"B1n{n}s{s}"
        par = (n * NST + s) % 2
        xr = f"xb{s % 2}"
        xb = (xb0, xb1)[s % 2]
        if nxt is not None:
            b1_ldrow(nxt)
            b1_load(nxt)
        bcast(rowpB[par][0:1, 512:1024], f"rowpB{par}", t_rs1[:], "t_rs1",
              "bT0", psT[0], "rs1" + tg)
        bcast(rowpB[par][0:1, 0:512], f"rowpB{par}", t_mrs1[:], "t_mrs1",
              "bT1", psT[1], "mrs" + tg)
        p.op("vector", lambda e: e.tensor_tensor(
            t_mrs1[:], t_mrs1[:], t_rs1[:], ALU.mult),
            reads=["t_mrs1", "t_rs1"], writes=["t_mrs1"], tag="mrsm" + tg)
        tt_bcast_split(cf, "cf", xb, xr, t_rs1, "t_rs1", ALU.mult, "am" + tg)
        tt_bcast_split(cf, "cf", cf, "cf", t_mrs1, "t_mrs1", ALU.subtract,
                       "as" + tg)
        for m in range(KC):
            pb, pr = ROT4[m % 3]
            pairs = [(Wo[:, j * 1024 + m * 128: j * 1024 + (m + 1) * 128],
                      yf[:, j * 512:(j + 1) * 512]) for j in range(KC)]
            mm_group(pr, pb[:], pairs, ["Wo", "yf"], f"o{m}" + tg)
            p.op("vector", lambda e, m=m, pb=pb: e.scalar_tensor_tensor(
                cf[:, m * 512:(m + 1) * 512], pb[:], sb["ob_col"][:, m:m + 1],
                cf[:, m * 512:(m + 1) * 512], ALU.add, ALU.add),
                reads=[pr, "ob_col", "cf"], writes=["cf"], tag=f"oe{m}" + tg)

    def b1_c(u):
        n, s = u
        tg = f"B1n{n}s{s}"
        if debug and n == 0 and s == 0:
            tp = tap("c", [128, KC * 512])
            p.op("sync", lambda e, tp=tp: e.dma_start(out=tp, in_=cf[:]),
                 reads=["cf"], dma=True, tag="tap_c")
        p.op("vector", lambda e: e.tensor_copy(z3b[:], cf[:]),
             reads=["cf"], writes=["z3b"], tag="ccast" + tg)
        p.op("scalar", lambda e: e.activation(sqb[:], cf[:], AF.Square),
             reads=["cf"], writes=["sqb"], tag="csq" + tg)
        mm_group("bT0", psT[0][0:1, 0:512],
                 [(ones_col[:], z3b[:, j * 512:(j + 1) * 512]) for j in range(KC)],
                 ["ones_col", "z3b"], "cssx" + tg)
        mm_group("bT1", psT[1][0:1, 0:512],
                 [(ones_col[:], sqb[:, j * 512:(j + 1) * 512]) for j in range(KC)],
                 ["ones_col", "sqb"], "cssq" + tg)
        ln_stat_rows("bT0", "bT1", psT[0], psT[1],
                     rs_out=(rowbf[0:1, 0:512], "rowbf"), tag="cln" + tg)
        bcast(rowbf[0:1, 0:512], "rowbf", t_rs3[:], "t_rs3", "bT0", psT[0],
              "rs3" + tg)

    def b1_h(u):
        n, s = u
        tg = f"B1n{n}s{s}"
        for m in range(KC):
            pb, pr = ROT4[m % 3]
            pairs = [(W1[:, j * 1024 + m * 128: j * 1024 + (m + 1) * 128],
                      z3b[:, j * 512:(j + 1) * 512]) for j in range(KC)]
            mm_group(pr, pb[:], pairs, ["W1", "z3b"], f"h{m}" + tg)
            p.op("vector", lambda e, m=m, pb=pb: e.tensor_tensor(
                sqb[:, m * 512:(m + 1) * 512], pb[:], t_rs3[:], ALU.mult),
                reads=[pr, "t_rs3", "sqb"], writes=["sqb"], tag=f"hm{m}" + tg)
            p.op("scalar", lambda e, m=m: e.activation(
                yf[:, m * 512:(m + 1) * 512], sqb[:, m * 512:(m + 1) * 512],
                AF.Relu, bias=bias1_col[:, m:m + 1]),
                reads=["sqb", "bias1_col"], writes=["yf"], tag=f"he{m}" + tg)

    def b1_g(u):
        n, s = u
        tg = f"B1n{n}s{s}"
        for m in range(KC):
            pb, pr = ROT4[m % 3]
            pairs = [(W2[:, j * 1024 + m * 128: j * 1024 + (m + 1) * 128],
                      yf[:, j * 512:(j + 1) * 512]) for j in range(KC)]
            mm_group(pr, pb[:], pairs, ["W2", "yf"], f"g{m}" + tg)
            p.op("vector", lambda e, m=m, pb=pb: e.scalar_tensor_tensor(
                cf[:, m * 512:(m + 1) * 512], pb[:], sb["b2a"][:, m:m + 1],
                cf[:, m * 512:(m + 1) * 512], ALU.add, ALU.add),
                reads=[pr, "b2a", "cf"], writes=["cf"], tag=f"ge{m}" + tg)

    def b1_l4(u):
        n, s = u
        tg = f"B1n{n}s{s}"
        xr = f"xb{s % 2}"
        xb = (xb0, xb1)[s % 2]
        p.op("vector", lambda e: e.tensor_copy(z3b[:], cf[:]),
             reads=["cf"], writes=["z3b"], tag="gcast" + tg)
        p.op("scalar", lambda e: e.activation(sqb[:], cf[:], AF.Square),
             reads=["cf"], writes=["sqb"], tag="gsq" + tg)
        mm_group("bT0", psT[0][0:1, 0:512],
                 [(ones_col[:], z3b[:, j * 512:(j + 1) * 512]) for j in range(KC)],
                 ["ones_col", "z3b"], "gssx" + tg)
        mm_group("bT1", psT[1][0:1, 0:512],
                 [(ones_col[:], sqb[:, j * 512:(j + 1) * 512]) for j in range(KC)],
                 ["ones_col", "sqb"], "gssq" + tg)
        ln_stat_rows("bT0", "bT1", psT[0], psT[1],
                     m_out=(rowbf[0:1, 512:1024], "rowbf"),
                     rs_out=(rowbf[0:1, 0:512], "rowbf"), tag="g4" + tg)
        bcast(rowbf[0:1, 0:512], "rowbf", t_rs4[:], "t_rs4", "bT0", psT[0],
              "rs4" + tg)
        bcast(rowbf[0:1, 512:1024], "rowbf", t_mrs4[:], "t_mrs4", "bT1", psT[1],
              "mrs4" + tg)
        p.op("vector", lambda e: e.tensor_tensor(
            t_mrs4[:], t_mrs4[:], t_rs4[:], ALU.mult),
            reads=["t_mrs4", "t_rs4"], writes=["t_mrs4"], tag="mrs4m" + tg)
        tt_bcast_split(cf, "cf", cf, "cf", t_rs4, "t_rs4", ALU.mult, "h1m" + tg)
        tt_bcast_split(cf, "cf", cf, "cf", t_mrs4, "t_mrs4", ALU.subtract,
                       "h1s" + tg)
        tt_plain_split(xb, xr, xb, xr, cf, "cf", ALU.add, "h1a" + tg)
        p.op("sync", lambda e, n=n, s=s, xb=xb: e.dma_start(
            out=h1_hbm[:, (n * NST + s) * 4096:(n * NST + s + 1) * 4096], in_=xb[:]),
            reads=[xr], writes=[f"h1st{n}_{s}"], dma=True, tag="sth1" + tg)
        if debug and n == 0 and s == 0:
            tp = tap("h1", [128, KC * 512])
            p.op("sync", lambda e, tp=tp, xb=xb: e.dma_start(out=tp, in_=xb[:]),
                 reads=[xr], dma=True, tag="tap_h1")

    b1_ldrow(B1U[0])
    b1_load(B1U[0])
    b1_q(B1U[0])
    b1_y(B1U[0])
    for i, u in enumerate(B1U):
        nxt = B1U[i + 1] if i + 1 < len(B1U) else None
        b1_o(u, nxt)
        if nxt is not None:
            b1_q(nxt)
        b1_c(u)
        b1_h(u)
        b1_g(u)
        if nxt is not None:
            b1_y(nxt)
        b1_l4(u)
    ctxB.close()
    p.barrier()

    ctxC = contextlib.ExitStack()
    U1s = alloc("U1s", [128, KC * 256], BF16, ctxC)
    V1s = alloc("V1s", [128, 2 * 4096], BF16, ctxC)
    U2s = alloc("U2s", [128, 32 * 256], BF16, ctxC)
    V2s = alloc("V2s", [128, 2 * 1024], BF16, ctxC)
    h1a = alloc("h1a", [128, KC * 512], F32, ctxC)
    h1b = alloc("h1b", [128, KC * 512], F32, ctxC)
    Hb = alloc("Hb", [128, 32 * 512], BF16, ctxC)
    rb2 = [alloc(f"rb2_{i}", [128, KC * 512], BF16, ctxC) for i in range(2)]
    hc = [alloc(f"hc{i}", [128, KC * 512], BF16, ctxC) for i in range(2)]
    sq2 = alloc("sq2", [128, KC * 512], BF16, ctxC)
    sq6 = alloc("sq6", [128, KC * 512], BF16, ctxC)
    Pbf = [alloc(f"Pbf{i}", [128, 2 * 512], BF16, ctxC) for i in range(2)]
    Qbf = [alloc(f"Qbf{i}", [128, 2 * 512], BF16, ctxC) for i in range(2)]
    t2_rs5 = alloc("t2_rs5", [128, 512], BF16, ctxC)
    t2_rs6 = alloc("t2_rs6", [128, 512], BF16, ctxC)
    t2_m6 = alloc("t2_m6", [128, 512], BF16, ctxC)
    t2_cs = alloc("t2_cs", [128, 256], BF16, ctxC)
    rowbf2 = alloc("rowbf2", [1, 1024], BF16, ctxC)
    brow2 = alloc("brow2", [1, 256], F32, ctxC)

    for qeng, (nm, t, src, w) in zip(
            ("gpsimd", "gpsimd", "gpsimd", "gpsimd"),
            (("U1s", U1s, U1, 256), ("V1s", V1s, V1, 4096),
             ("U2s", U2s, U2, 256), ("V2s", V2s, V2, 1024))):
        r3 = src.rearrange("(j p) f -> p j f", p=128)
        p.op(qeng, lambda e, t=t, r3=r3, w=w: e.dma_start(
            out=t[:].rearrange("p (j f) -> p j f", f=w), in_=r3),
            writes=[nm], dma=True, tag="ld_" + nm)
    pairs = [(ones_col[:], U1s[:, j * 256:(j + 1) * 256]) for j in range(KC)]
    mm_group("bT0", psT[0][0:1, 0:256], pairs, ["ones_col", "U1s"], "u1cs")
    p.op("vector", lambda e: e.tensor_scalar_mul(brow2[:], psT[0][0:1, 0:256], 1.0 / C),
         reads=["bT0"], writes=["brow2"], tag="u1cse")
    p.op("vector", lambda e: e.tensor_copy(rowbf2[0:1, 0:256], brow2[:]),
         reads=["brow2"], writes=["rowbf2"], tag="u1csbf")
    p.op("tensor", lambda e: e.matmul(psT[0][:, 0:256], ones_row[:],
                                      rowbf2[0:1, 0:256], start=True, stop=True,
                                      skip_group_check=True),
         reads=["ones_row", "rowbf2"], writes=["bT0"], tag="u1cb")
    p.op("scalar", lambda e: e.copy(t2_cs[:], psT[0][:, 0:256]),
         reads=["bT0"], writes=["t2_cs"], tag="u1cbe")
    for j in range(KC):
        p.op("vector", lambda e, j=j: e.tensor_tensor(
            U1s[:, j * 256:(j + 1) * 256], U1s[:, j * 256:(j + 1) * 256],
            t2_cs[:], ALU.subtract),
            reads=["U1s", "t2_cs"], writes=["U1s"], tag=f"u1c{j}")

    B2U = [(n, s) for n in range(NSEQ) for s in range(NST)]

    def b2_load(u):
        n, s = u
        hb = (h1a, h1b)[s % 2]
        p.op("sync", lambda e, n=n, s=s, hb=hb: e.dma_start(
            out=hb[:], in_=h1_hbm[:, (n * NST + s) * 4096:(n * NST + s + 1) * 4096]),
            reads=[f"h1st{n}_{s}"], writes=[f"h1{s % 2}"], dma=True,
            tag=f"ldh1B2n{n}s{s}")
        p.op("gpsimd", lambda e, n=n, s=s: e.dma_start(
            out=hc[s % 2][:],
            in_=h1_hbm[:, (n * NST + s) * 4096:(n * NST + s + 1) * 4096]),
            reads=[f"h1st{n}_{s}"], writes=[f"hc{s % 2}"], dma=True,
            tag=f"ldhcB2n{n}s{s}")

    def b2_head(u):
        n, s = u
        tg = f"B2n{n}s{s}"
        hb = (h1a, h1b)[s % 2]
        hr = f"h1{s % 2}"
        hcb, hcr = hc[s % 2], f"hc{s % 2}"
        pp = s % 2
        for m in range(2):
            pairs = [(U1s[:, j * 256 + m * 128: j * 256 + (m + 1) * 128],
                      hcb[:, j * 512:(j + 1) * 512]) for j in range(KC)]
            mm_group(f"bA{m}", psA[m][:], pairs, ["U1s", hcr], f"P{m}" + tg)
        p.op("vector", lambda e, hb=hb: e.tensor_tensor(
            sq2[:], hb[:], hb[:], ALU.mult),
            reads=[hr], writes=["sq2"], tag="h1sq" + tg)
        mm_group("bT0", psT[0][0:1, 0:512],
                 [(ones_col[:], hcb[:, j * 512:(j + 1) * 512]) for j in range(KC)],
                 ["ones_col", hcr], "h1ssx" + tg)
        mm_group("bT1", psT[1][0:1, 0:512],
                 [(ones_col[:], sq2[:, j * 512:(j + 1) * 512]) for j in range(KC)],
                 ["ones_col", "sq2"], "h1ssq" + tg)
        ln_stat_rows("bT0", "bT1", psT[0], psT[1],
                     rs_out=(rowbf2[0:1, 0:512], "rowbf2"), tag="l5" + tg)
        bcast(rowbf2[0:1, 0:512], "rowbf2", t2_rs5[:], "t2_rs5", "bT0", psT[0],
              "rs5" + tg)
        for m in range(2):
            p.op("vector", lambda e, m=m, pp=pp: e.tensor_tensor(
                Pbf[pp][:, m * 512:(m + 1) * 512], psA[m][:], t2_rs5[:],
                ALU.mult),
                reads=[f"bA{m}", "t2_rs5"], writes=[f"Pbf{pp}"], tag=f"Pe{m}" + tg)

    ROTH = [(psB[0], "bB0"), (psT[0], "bT0"), (psT[1], "bT1")]

    def b2_body(u):
        n, s = u
        tg = f"B2n{n}s{s}"
        pp = s % 2
        DEPTH = 4
        def qmm(mh):
            for mq in range(2):
                mm_group(f"bS{mq}", psS[mq][:],
                         [(U2s[:, mh * 256 + mq * 128: mh * 256 + (mq + 1) * 128],
                           Hb[:, mh * 512:(mh + 1) * 512])],
                         ["U2s", f"Hb{mh % 8}"], f"Q{mh}.{mq}" + tg,
                         start=(mh == 0), stop=(mh == 31))
        for mh in range(32):
            pb, pr = ROTH[mh % 3]
            pairs = [(V1s[:, j * 4096 + mh * 128: j * 4096 + (mh + 1) * 128],
                      Pbf[pp][:, j * 512:(j + 1) * 512]) for j in range(2)]
            mm_group(pr, pb[:], pairs, ["V1s", f"Pbf{pp}"], f"H{mh}" + tg)
            p.op("scalar", lambda e, mh=mh, pb=pb: e.activation(
                Hb[:, mh * 512:(mh + 1) * 512], pb[:], AF.Gelu_apprx_tanh,
                bias=sb["fb1"][:, mh:mh + 1]),
                reads=[pr, "fb1"], writes=[f"Hb{mh % 8}"], tag=f"He{mh}" + tg)
            if mh >= DEPTH:
                qmm(mh - DEPTH)
        for mh in range(32 - DEPTH, 32):
            qmm(mh)
        for mq in range(2):
            p.op("vector", lambda e, mq=mq, pp=pp: e.tensor_copy(
                Qbf[pp][:, mq * 512:(mq + 1) * 512], psS[mq][:]),
                reads=[f"bS{mq}"], writes=[f"Qbf{pp}"], tag=f"Qe{mq}" + tg)
        for m in range(KC):
            pb, pr = ROTH[m % 3]
            pairs = [(V2s[:, j * 1024 + m * 128: j * 1024 + (m + 1) * 128],
                      Qbf[pp][:, j * 512:(j + 1) * 512]) for j in range(2)]
            mm_group(pr, pb[:], pairs, ["V2s", f"Qbf{pp}"], f"r{m}" + tg)
            p.op("vector", lambda e, m=m, pb=pb, pp=pp: e.tensor_scalar_add(
                rb2[pp][:, m * 512:(m + 1) * 512], pb[:],
                sb["fb2"][:, m:m + 1]),
                reads=[pr, "fb2"], writes=[f"rb2_{pp}"], tag=f"re{m}" + tg)

    def b2_tail(u):
        n, s = u
        tg = f"B2n{n}s{s}"
        pp = s % 2
        hb = (h1a, h1b)[s % 2]
        hr = f"h1{s % 2}"
        rb = rb2[pp]
        rr = f"rb2_{pp}"
        if debug and n == 0 and s == 0:
            tp = tap("r", [128, KC * 512], BF16)
            p.op("sync", lambda e, tp=tp: e.dma_start(out=tp, in_=rb[:]),
                 reads=[rr], dma=True, tag="tap_r")
        p.op("vector", lambda e: e.tensor_tensor(sq6[:], rb[:], rb[:], ALU.mult),
             reads=[rr], writes=["sq6"], tag="rsq" + tg)
        mm_group("bT0", psT[0][0:1, 0:512],
                 [(ones_col[:], rb[:, j * 512:(j + 1) * 512]) for j in range(KC)],
                 ["ones_col", rr], "rssx" + tg)
        mm_group("bT1", psT[1][0:1, 0:512],
                 [(ones_col[:], sq6[:, j * 512:(j + 1) * 512]) for j in range(KC)],
                 ["ones_col", "sq6"], "rssq" + tg)
        ln_stat_rows("bT0", "bT1", psT[0], psT[1],
                     m_out=(rowbf2[0:1, 512:1024], "rowbf2"),
                     rs_out=(rowbf2[0:1, 0:512], "rowbf2"), tag="l6" + tg)
        bcast(rowbf2[0:1, 0:512], "rowbf2", t2_rs6[:], "t2_rs6", "bT0", psT[0],
              "rs6" + tg)
        bcast(rowbf2[0:1, 512:1024], "rowbf2", t2_m6[:], "t2_m6", "bT1", psT[1],
              "mrs6" + tg)
        p.op("vector", lambda e: e.tensor_tensor(
            t2_m6[:], t2_m6[:], t2_rs6[:], ALU.mult),
            reads=["t2_m6", "t2_rs6"], writes=["t2_m6"], tag="mrs6m" + tg)
        tt_bcast_split(rb, rr, rb, rr, t2_rs6, "t2_rs6", ALU.mult,
                       "om" + tg)
        tt_bcast_split(rb, rr, rb, rr, t2_m6, "t2_m6", ALU.subtract,
                       "os" + tg)
        tt_plain_split(hb, hr, hb, hr, rb, rr, ALU.add, "oa" + tg)
        p.op("sync", lambda e, n=n, s=s, hb=hb: e.dma_start(
            out=yT3[:, n * KC:(n + 1) * KC, s * ST:s * ST + 512],
            in_=hb[:].rearrange("p (j t) -> p j t", t=512)),
            reads=[hr], dma=True, tag="sty" + tg)

    b2_load(B2U[0])
    b2_head(B2U[0])
    for i, u in enumerate(B2U):
        nxt = B2U[i + 1] if i + 1 < len(B2U) else None
        b2_body(u)
        if nxt is not None:
            b2_load(nxt)
            b2_head(nxt)
        b2_tail(u)
    ctxC.close()

    p.finalize()
    p.emit(ctx)
    return nc, ctx, taps


def _col(v):
    v = np.asarray(v, dtype=np.float32).reshape(-1)
    return np.ascontiguousarray(v.reshape(-1, 128).T)


def kernel(L0_attn_x, L0_attn_dn_g, L0_attn_dn_b, L0_attn_gates_w, L0_attn_gates_b,
           L0_attn_lr_w, L0_attn_lr_b, L0_attn_out_w, L0_attn_out_b, L0_attn_mn_g,
           L0_attn_mn_b, L0_attn_w1, L0_attn_b1, L0_attn_w2, L0_attn_b2,
           L0_ffn_U1, L0_ffn_V1, L0_ffn_b1, L0_ffn_U2, L0_ffn_V2, L0_ffn_b2,
           _debug=False):
    x = np.asarray(L0_attn_x, dtype=np.float32)
    N = x.shape[0]
    assert x.shape == (N, T, C) and N == NCORE * NSEQ
    xT = np.ascontiguousarray(x.transpose(0, 2, 1))

    f32 = lambda a: np.ascontiguousarray(np.asarray(a, dtype=np.float32))
    common = {
        "gates_w": f32(L0_attn_gates_w), "out_w": f32(L0_attn_out_w),
        "w1": f32(L0_attn_w1), "w2": f32(L0_attn_w2),
        "U1": f32(L0_ffn_U1), "V1": f32(L0_ffn_V1),
        "U2": f32(L0_ffn_U2), "V2": f32(L0_ffn_V2),
        "dn_g": _col(L0_attn_dn_g), "dn_b": _col(L0_attn_dn_b),
        "mn_g": _col(L0_attn_mn_g), "mn_b": _col(L0_attn_mn_b),
        "gates_b_col": _col(L0_attn_gates_b), "out_b_col": _col(L0_attn_out_b),
        "b1a_col": _col(L0_attn_b1), "b2a_col": _col(L0_attn_b2),
        "fb1_col": _col(L0_ffn_b1), "fb2_col": _col(L0_ffn_b2),
        "lrw_col": _col(L0_attn_lr_w),
        "lrb_in": f32(L0_attn_lr_b).reshape(1, 1),
        "gates_b_row": f32(L0_attn_gates_b).reshape(1, 3 * C),
        "ident_in": np.eye(128, dtype=np.float32),
        "maskL_in": np.tril(np.ones((128, 128), np.float32), -1),
        "maskU_in": np.triu(np.ones((128, 128), np.float32), 1),
    }
    nc, ctx, taps = build_program(debug=_debug)
    in_maps = []
    for i in range(NCORE):
        m = dict(common)
        m["xT"] = np.ascontiguousarray(xT[i * NSEQ:(i + 1) * NSEQ])
        in_maps.append(m)
    res = run_bass_kernel_spmd(nc, in_maps, core_ids=list(range(NCORE)))
    ctx.close()
    out = np.empty((N, T, C), dtype=np.float32)
    for i in range(NCORE):
        yt = np.asarray(res.results[i]["yT"])
        out[i * NSEQ:(i + 1) * NSEQ] = yt.transpose(0, 2, 1)
    if _debug:
        return out, {k: np.asarray(res.results[0]["tap_" + k]) for k in taps}
    return out


# version 38 (final)
# speedup vs baseline: 1.0681x; 1.0681x over previous
import collections
import contextlib
import numpy as np
import ml_dtypes
import concourse.bass as bass
import concourse.mybir as mybir
from concourse.bass_utils import run_bass_kernel_spmd

F32 = mybir.dt.float32
BF16 = mybir.dt.bfloat16
AF = mybir.ActivationFunctionType
ALU = mybir.AluOpType
AX = mybir.AxisListType

EPS = 1e-5
C = 1024
KC = 8
T = 4096
ST = 512
NST = T // ST
CH = 4
NSEQ = 2
NCORE = 8

ENGINES = ("sync", "gpsimd", "vector", "scalar", "tensor")


class Prog:

    def __init__(self, nc):
        self.nc = nc
        self.ops = []
        self.res = {}
        self.cnt = collections.defaultdict(int)

    def _st(self, key):
        if key not in self.res:
            self.res[key] = {"w": [], "r": []}
        return self.res[key]

    SELF_ORDERED = set()

    def op(self, eng, fn, reads=(), writes=(), dma=False, tag="", extra_waits=()):
        kind = "dma" if dma else "cmp"
        ev = (eng, kind, self.cnt[(eng, kind)])
        self.cnt[(eng, kind)] += 1
        waits = set(extra_waits)
        for r in reads:
            st = self._st(r)
            waits.update(st["w"])
        for w in writes:
            st = self._st(w)
            waits.update(st["w"])
            waits.update(st["r"])
        waits.discard(ev)
        if (eng, kind) in self.SELF_ORDERED:
            waits = {x for x in waits if (x[0], x[1]) != (eng, kind)}
        self.ops.append(dict(ev=ev, fn=fn, waits=waits, tag=tag))
        for r in reads:
            self._st(r)["r"].append(ev)
        for w in writes:
            self.res[w] = {"w": [ev], "r": []}
        return ev

    def par(self, specs):
        snap = {}
        keys = set()
        for _, _, reads, writes, _ in specs:
            keys |= set(reads) | set(writes)
        for k in keys:
            st = self._st(k)
            snap[k] = (list(st["w"]), list(st["r"]))
        evs = []
        wrote = {}
        for eng, fn, reads, writes, tag in specs:
            kind = "cmp"
            ev = (eng, kind, self.cnt[(eng, kind)])
            self.cnt[(eng, kind)] += 1
            waits = set()
            for r in reads:
                waits.update(snap[r][0])
            for w in writes:
                waits.update(snap[w][0])
                waits.update(snap[w][1])
            waits.discard(ev)
            self.ops.append(dict(ev=ev, fn=fn, waits=waits, tag=tag))
            evs.append(ev)
            for r in reads:
                self._st(r)["r"].append(ev)
            for w in writes:
                wrote.setdefault(w, []).append(ev)
        for w, wl in wrote.items():
            self.res[w] = {"w": list(wl), "r": []}
        return evs

    def last_evs(self):
        return [(e, k, n - 1) for (e, k), n in self.cnt.items() if n > 0]

    def barrier(self, engines=ENGINES):
        evs = self.last_evs()
        for eng in engines:
            self.op(eng, lambda e: e.nop(), extra_waits=evs, tag="barrier")

    def finalize(self):
        evs = self.last_evs()
        self.op("sync", lambda e: e.nop(), extra_waits=evs, tag="final")

    def emit(self, exitstack):
        nc = self.nc
        needed = set()
        for o in self.ops:
            needed |= o["waits"]
        for o in self.ops:
            if o["ev"][1] == "dma":
                needed.add(o["ev"])
        incval = {}
        run = collections.defaultdict(int)
        for o in self.ops:
            eng, kind, _ = o["ev"]
            if o["ev"] in needed:
                run[(eng, kind)] += 16 if kind == "dma" else 1
                incval[o["ev"]] = run[(eng, kind)]
        sems = {}
        for key, total in run.items():
            if total > 0:
                sems[key] = exitstack.enter_context(
                    nc.semaphore(f"s_{key[0]}_{key[1]}"))
        for o in self.ops:
            for w in o["waits"]:
                assert w in incval, (w, o["tag"])
        per_eng = collections.defaultdict(list)
        for o in self.ops:
            per_eng[o["ev"][0]].append(o)
        block = exitstack.enter_context(nc.Block())
        watermark = collections.defaultdict(int)

        warm = getattr(self, "warm", None)

        def runner(eng_name):
            def go(engine):
                for o in per_eng[eng_name]:
                    bysem = {}
                    for w in o["waits"]:
                        k = (w[0], w[1])
                        v = incval[w]
                        if v > bysem.get(k, 0):
                            bysem[k] = v
                    emitted_wait = False
                    for k, v in sorted(bysem.items()):
                        if v > watermark[(eng_name, k)]:
                            if (eng_name == "tensor" and warm is not None
                                    and not emitted_wait):
                                warm(engine)
                                emitted_wait = True
                            engine.wait_ge(sems[k], v)
                            watermark[(eng_name, k)] = v
                    ins = o["fn"](engine)
                    if o["ev"] in incval:
                        eng, kind, _ = o["ev"]
                        ins.then_inc(sems[(eng, kind)], 16 if kind == "dma" else 1)
            return go

        for eng_name in ENGINES:
            if per_eng[eng_name]:
                getattr(block, eng_name)(runner(eng_name))
        return sems


def build_program(debug=False):
    nc = bass.Bass("TRN2", target_bir_lowering=False, debug=False)
    dram = {}

    def din(name, shape, dt=F32):
        dram[name] = nc.dram_tensor(name, shape, dt, kind="ExternalInput").ap()
        return dram[name]

    xT = din("xT", [NSEQ, C, T])
    gates_w = din("gates_w", [C, 3 * C])
    out_w = din("out_w", [C, C])
    w1 = din("w1", [C, C])
    w2 = din("w2", [C, C])
    U1 = din("U1", [C, 256])
    V1 = din("V1", [256, 4096])
    U2 = din("U2", [4096, 256])
    V2 = din("V2", [256, C])
    dn_g = din("dn_g", [128, 8])
    dn_b = din("dn_b", [128, 8])
    mn_g = din("mn_g", [128, 8])
    mn_b = din("mn_b", [128, 8])
    gates_b_col = din("gates_b_col", [128, 24])
    out_b_col = din("out_b_col", [128, 8])
    b1a_col = din("b1a_col", [128, 8])
    b2a_col = din("b2a_col", [128, 8])
    fb1_col = din("fb1_col", [128, 32])
    fb2_col = din("fb2_col", [128, 8])
    lrw_col = din("lrw_col", [128, 8])
    lrb_in = din("lrb_in", [1, 1])
    gates_b_row = din("gates_b_row", [1, 3 * C])
    ident_in = din("ident_in", [128, 128])
    maskL_in = din("maskL_in", [128, 128])
    maskU_in = din("maskU_in", [128, 128])

    yT = nc.dram_tensor("yT", [NSEQ, C, T], F32, kind="ExternalOutput").ap()

    Wstash = nc.dram_tensor("Wstash", [128, NSEQ * 8192], BF16, kind="Internal").ap()
    rows_hbm = nc.dram_tensor("rows_hbm", [1, NSEQ * NST * 2048], BF16, kind="Internal").ap()
    h1_hbm = nc.dram_tensor("h1_hbm", [128, NSEQ * T * KC], F32, kind="Internal").ap()

    taps = {}

    def tap(name, shape, dt=F32):
        if not debug:
            return None
        taps[name] = nc.dram_tensor("tap_" + name, shape, dt, kind="ExternalOutput").ap()
        return taps[name]

    p = Prog(nc)
    ctx = contextlib.ExitStack()
    E = ctx.enter_context

    sb = {}

    def alloc(name, shape, dt, stack=None):
        t = (stack or ctx).enter_context(nc.sbuf_tensor("sb_" + name, shape, dt))
        sb[name] = t
        return t

    for nm, src in (("dn_g", dn_g), ("dn_b", dn_b), ("mn_g", mn_g), ("mn_b", mn_b),
                    ("gb_col", gates_b_col), ("ob_col", out_b_col),
                    ("b1a", b1a_col), ("b2a", b2a_col), ("fb1", fb1_col),
                    ("fb2", fb2_col), ("lrw_f", lrw_col)):
        t = alloc(nm, list(src.shape), F32)
        p.op("sync", lambda e, t=t, src=src: e.dma_start(out=t[:], in_=src),
             writes=[nm], dma=True, tag="ld_" + nm)
    lrb_sb = alloc("lrb", [1, 1], F32)
    p.op("sync", lambda e: e.dma_start(out=lrb_sb[:], in_=lrb_in), writes=["lrb"], dma=True)
    ident_bf = alloc("ident_bf", [128, 128], BF16)
    maskL = alloc("maskL", [128, 128], BF16)
    maskU = alloc("maskU", [128, 128], BF16)
    for t, src, nm in ((ident_bf, ident_in, "ident"), (maskL, maskL_in, "maskL"),
                       (maskU, maskU_in, "maskU")):
        p.op("gpsimd", lambda e, t=t, src=src: e.dma_start(out=t[:], in_=src),
             writes=[nm], dma=True, tag="ld_" + nm)
    ones_col = alloc("ones_col", [128, 1], BF16)
    p.op("gpsimd", lambda e: e.memset(ones_col[:], 1.0), writes=["ones_col"])
    ones_row = alloc("ones_row", [1, 128], BF16)
    p.op("gpsimd", lambda e: e.memset(ones_row[:], 1.0), writes=["ones_row"])
    one1_f = alloc("one1_f", [1, 1], F32)
    p.op("gpsimd", lambda e: e.memset(one1_f[:], 1.0), writes=["one1"])
    eps1 = alloc("eps1", [1, 1], F32)
    p.op("gpsimd", lambda e: e.memset(eps1[:], EPS), writes=["eps1"])
    eps1b = alloc("eps1b", [1, 1], F32)
    p.op("gpsimd", lambda e: e.memset(eps1b[:], 1.0 + EPS), writes=["eps1b"])
    eps2 = alloc("eps2", [1, 1], F32)
    p.op("gpsimd", lambda e: e.memset(eps2[:], EPS * EPS), writes=["eps2"])
    neghalf = alloc("neghalf", [1, 1], F32)
    p.op("gpsimd", lambda e: e.memset(neghalf[:], -0.5), writes=["neghalf"])
    dnb_bf = alloc("dnb_bf", [128, 8], BF16)
    p.op("vector", lambda e: e.tensor_copy(dnb_bf[:], sb["dn_b"][:]),
         reads=["dn_b"], writes=["dnb_bf"])
    mnb_bf = alloc("mnb_bf", [128, 8], BF16)
    p.op("vector", lambda e: e.tensor_copy(mnb_bf[:], sb["mn_b"][:]),
         reads=["mn_b"], writes=["mnb_bf"])
    biasq_col = alloc("biasq_col", [128, 8], F32)
    bias1_col = alloc("bias1_col", [128, 8], F32)
    lrb_col = alloc("lrb_col", [128, 1], F32)

    psA = [E(nc.psum_tensor(f"psA{i}", [128, 512], F32)) for i in range(2)]
    psS = [E(nc.psum_tensor(f"psS{i}", [128, 512], F32)) for i in range(2)]
    psT = [E(nc.psum_tensor(f"psT{i}", [128, 512], F32)) for i in range(2)]
    psB = [E(nc.psum_tensor(f"psB{i}", [128, 512], F32)) for i in range(1)]
    psW = E(nc.psum_tensor("psW", [128, 512], F32))
    ROT4 = [(psA[0], "bA0"), (psA[1], "bA1"), (psB[0], "bB0")]
    ROTS = [(psS[0], "bS0"), (psS[1], "bS1"), (psA[0], "bA0"), (psA[1], "bA1")]

    def _warm(eng):
        eng.matmul(psW[0:1, 0:1], ones_row[0:1, 0:1], ones_row[0:1, 0:1],
                   start=True, stop=True, skip_group_check=True)
    p.warm = None

    def mm_group(bank_res, psum_ap, pairs, reads, tag, start=True, stop=True):
        def fn(e, pairs=pairs, psum_ap=psum_ap):
            ins = None
            n = len(pairs)
            for i, (lhsT, rhs) in enumerate(pairs):
                ins = e.matmul(psum_ap, lhsT, rhs,
                               start=(start and i == 0), stop=(stop and i == n - 1),
                               skip_group_check=True)
            return ins
        return p.op("tensor", fn, reads=list(reads), writes=[bank_res], tag=tag)

    def bcast(row_ap, row_res, tile_ap, tile_res, bank_res, ps, tag):
        p.op("tensor", lambda e: e.matmul(ps[:, 0:512], ones_row[:], row_ap,
                                          start=True, stop=True, skip_group_check=True),
             reads=["ones_row", row_res], writes=[bank_res], tag="bc_" + tag)
        p.op("scalar", lambda e: e.copy(tile_ap, ps[:, 0:512]),
             reads=[bank_res], writes=[tile_res], tag="bce_" + tag)

    rowa = alloc("rowa", [1, 512 * 6], F32)
    R_M, R_E2, R_T, R_RS, R_T2, R_X = (rowa[:][0:1, i * 512:(i + 1) * 512] for i in range(6))

    def ln_stat_rows(sx_res, sxx_res, ps_sx, ps_sxx, m_out=None, rs_out=None,
                     rs12_out=None, tag=""):
        p.op("vector", lambda e: e.tensor_scalar_mul(R_M, ps_sx[0:1, 0:512], 1.0 / C),
             reads=[sx_res], writes=["r_m"], tag="rm" + tag)
        p.op("vector", lambda e: e.tensor_scalar_mul(R_E2, ps_sxx[0:1, 0:512], 1.0 / C),
             reads=[sxx_res], writes=["r_e2"], tag="re2" + tag)
        p.op("vector", lambda e: e.tensor_tensor(R_T, R_M, R_M, ALU.mult),
             reads=["r_m"], writes=["r_t"], tag="rmsq" + tag)
        p.op("vector", lambda e: e.scalar_tensor_tensor(
            R_T, R_E2, 1.0, R_T, ALU.mult, ALU.subtract),
            reads=["r_e2", "r_t"], writes=["r_t"], tag="rvar" + tag)
        if m_out is not None:
            ap, res = m_out
            p.op("vector", lambda e, ap=ap: e.tensor_copy(ap, R_M),
                 reads=["r_m"], writes=[res], tag="rmo" + tag)
        if rs_out is not None:
            ap, res = rs_out
            p.op("scalar", lambda e: e.activation(R_T2, R_T, AF.Ln, bias=eps1[:]),
                 reads=["r_t", "eps1"], writes=["r_t2"], tag="rln" + tag)
            p.op("scalar", lambda e, ap=ap: e.activation(ap, R_T2, AF.Exp,
                                                         scale=neghalf[:]),
                 reads=["r_t2", "neghalf"], writes=[res], tag="rrs" + tag)
        if rs12_out is not None:
            ap, res = rs12_out
            p.op("scalar", lambda e: e.activation(
                R_X, R_T, AF.Ln, bias=eps2[:], scale=eps1b[:]),
                reads=["r_t", "eps2", "eps1b"], writes=["r_x"], tag="rln12" + tag)
            p.op("scalar", lambda e, ap=ap: e.activation(ap, R_X, AF.Exp,
                                                         scale=neghalf[:]),
                 reads=["r_x", "neghalf"], writes=[res], tag="rrs12" + tag)

    POOL_SPLIT = KC

    def tt_bcast_split(out_t, out_res, in0_t, in0_res, tile_t, tile_res, op, tag):
        specs = []
        for eng, j0, j1 in (("vector", 0, POOL_SPLIT), ("gpsimd", POOL_SPLIT, KC)):
            if j1 <= j0:
                continue
            o3 = out_t[:].rearrange("p (j t) -> p j t", t=512)[:, j0:j1, :]
            i3 = in0_t[:].rearrange("p (j t) -> p j t", t=512)[:, j0:j1, :]
            b3 = tile_t[:].unsqueeze(1).broadcast_to((128, j1 - j0, 512))
            specs.append((eng,
                          (lambda e, o3=o3, i3=i3, b3=b3, op=op:
                           e.tensor_tensor(o3, i3, b3, op)),
                          [in0_res, tile_res], [out_res], f"{tag}.{eng[0]}"))
        p.par(specs)

    def tt_plain_split(out_t, out_res, in0_t, in0_res, in1_t, in1_res, op, tag):
        specs = []
        for eng, j0, j1 in (("vector", 0, POOL_SPLIT), ("gpsimd", POOL_SPLIT, KC)):
            if j1 <= j0:
                continue
            sl = slice(j0 * 512, j1 * 512)
            specs.append((eng,
                          (lambda e, sl=sl, op=op: e.tensor_tensor(
                              out_t[:, sl], in0_t[:, sl], in1_t[:, sl], op)),
                          [in0_res, in1_res], [out_res], f"{tag}.{eng[0]}"))
        p.par(specs)

    gw3 = gates_w.rearrange("(j p) f -> p j f", p=128)
    xT3 = xT.rearrange("n (j p) t -> p (n j) t", p=128)
    yT3 = yT.rearrange("n (j p) t -> p (n j) t", p=128)

    ctxA = contextlib.ExitStack()
    WA = alloc("WA", [128, KC * 2048], BF16, ctxA)
    lrw_bf = alloc("lrw_bf", [128, 8], BF16, ctxA)
    Sf = alloc("Sf", [128, 8192], F32, ctxA)
    Wf = alloc("Wf", [128, 8192], F32, ctxA)
    Sbf = alloc("Sbf", [128, 8192], BF16, ctxA)
    xa = alloc("xa", [128, KC * 512], BF16, ctxA)
    z2a = alloc("z2a", [128, KC * 512], BF16, ctxA)
    v_tm = alloc("v_tm", [128, CH * 1024], BF16, ctxA)
    k_tm = alloc("k_tm", [128, CH * 1024], BF16, ctxA)
    u_tm = alloc("u_tm", [128, CH * 1024], BF16, ctxA)
    ulr_tm = alloc("ulr_tm", [128, CH * 1024], BF16, ctxA)
    k_fm = alloc("k_fm", [128, CH * 1024], BF16, ctxA)
    ksq = [alloc(f"ksq{i}", [128, 1024], BF16, ctxA) for i in range(2)]
    alpha_col = alloc("alpha_col", [128, 2 * CH], F32, ctxA)
    csb = [alloc(f"csb{i}", [128, 512], BF16, ctxA) for i in range(2)]
    lrm_col = alloc("lrm_col", [128, 1], F32, ctxA)
    Rbf = alloc("Rbf", [128, 1024], BF16, ctxA)
    tbX = [alloc(f"tbX{i}", [128, 128], F32, ctxA) for i in range(2)]
    tbZ = [alloc(f"tbZ{i}", [128, 128], F32, ctxA) for i in range(2)]
    tbA = [alloc(f"tbA{i}", [128, 128], F32, ctxA) for i in range(2)]
    tbAb = [alloc(f"tbAb{i}", [128, 128], BF16, ctxA) for i in range(CH)]
    tbD = [alloc(f"tbD{i}", [128, 128], BF16, ctxA) for i in range(2)]
    lrc = alloc("lrc", [128, CH], F32, ctxA)
    ksum = alloc("ksum", [128, CH], F32, ctxA)
    krs = alloc("krs", [128, CH], F32, ctxA)
    rowpA = alloc("rowpA", [1, 2048], BF16, ctxA)
    gbk_t = alloc("gbk_t", [128, 1024], F32, ctxA)
    gbv_t = alloc("gbv_t", [128, 1024], F32, ctxA)
    browkv = rowa[:][0:1, 0:2048]
    BROWKV_KEYS = ["r_m", "r_e2", "r_t", "r_rs"]
    browkv_bf = rowpA[:][0:1, 0:2048]

    p.op("gpsimd", lambda e: e.dma_start(
        out=WA[:].rearrange("p (j f) -> p j f", f=2048), in_=gw3[:, :, 1024:3072]),
        writes=["WA"], dma=True, tag="ld_WA")
    p.op("gpsimd", lambda e: e.dma_start(out=lrw_bf[:], in_=lrw_col),
         writes=["lrw_bf"], dma=True, tag="ld_lrw")
    p.op("sync", lambda e: e.dma_start(out=browkv, in_=gates_b_row[:, 1024:3072]),
         writes=BROWKV_KEYS, dma=True, tag="ld_gbrow")
    for i in range(4):
        pairs = [(dnb_bf[:, j:j + 1], WA[:, j * 2048 + i * 512: j * 2048 + (i + 1) * 512])
                 for j in range(KC)]
        mm_group("bT0", psT[0][0:1, 0:512], pairs, ["dnb_bf", "WA"], f"browkv{i}")
        p.op("vector", lambda e, i=i: e.tensor_tensor(
            browkv[0:1, i * 512:(i + 1) * 512], psT[0][0:1, 0:512],
            browkv[0:1, i * 512:(i + 1) * 512], ALU.add),
            reads=["bT0"] + BROWKV_KEYS, writes=BROWKV_KEYS, tag=f"browkv_add{i}")
    p.op("vector", lambda e: e.tensor_copy(browkv_bf, browkv),
         reads=BROWKV_KEYS, writes=["rowpA"])
    for i in range(2):
        bcast(browkv_bf[0:1, i * 512:(i + 1) * 512], "rowpA",
              gbk_t[:, i * 512:(i + 1) * 512], "gbk_t", "bT0", psT[0], f"gbk{i}")
    for i in range(2):
        bcast(browkv_bf[0:1, 1024 + i * 512: 1024 + (i + 1) * 512], "rowpA",
              gbv_t[:, i * 512:(i + 1) * 512], "gbv_t", "bT0", psT[0], f"gbv{i}")
    for j in range(KC):
        p.op("vector", lambda e, j=j: e.tensor_scalar_mul(
            WA[:, j * 2048:(j + 1) * 2048], WA[:, j * 2048:(j + 1) * 2048],
            sb["dn_g"][:, j:j + 1]),
            reads=["WA", "dn_g"], writes=["WA"], tag=f"foldWA{j}")
    p.op("vector", lambda e: e.tensor_tensor(lrw_bf[:], lrw_bf[:], sb["dn_g"][:], ALU.mult),
         reads=["lrw_bf", "dn_g"], writes=["lrw_bf"], tag="fold_lr")
    for i in range(4):
        pairs = [(ones_col[:], WA[:, j * 2048 + i * 512: j * 2048 + (i + 1) * 512])
                 for j in range(KC)]
        mm_group("bT0", psT[0][0:1, 0:512], pairs, ["ones_col", "WA"], f"wacs{i}")
        p.op("vector", lambda e, i=i: e.tensor_scalar_mul(
            rowa[:][0:1, i * 512:(i + 1) * 512], psT[0][0:1, 0:512], 1.0 / C),
            reads=["bT0"], writes=[BROWKV_KEYS[i]], tag=f"wacse{i}")
    for i in range(4):
        p.op("vector", lambda e, i=i: e.tensor_copy(
            csb[i % 2][:][0:1, 0:512], rowa[:][0:1, i * 512:(i + 1) * 512]),
            reads=[BROWKV_KEYS[i]], writes=[f"csb{i % 2}"], tag=f"wacsb{i}")
        p.op("tensor", lambda e, i=i: e.matmul(
            psT[0][:, 0:512], ones_row[:], csb[i % 2][:][0:1, 0:512],
            start=True, stop=True, skip_group_check=True),
            reads=["ones_row", f"csb{i % 2}"], writes=["bT0"], tag=f"wacb{i}")
        p.op("scalar", lambda e, i=i: e.copy(csb[i % 2][:], psT[0][:, 0:512]),
             reads=["bT0"], writes=[f"csb{i % 2}"], tag=f"wacbe{i}")
        for j in range(KC):
            p.op("vector", lambda e, i=i, j=j: e.tensor_tensor(
                WA[:, j * 2048 + i * 512: j * 2048 + (i + 1) * 512],
                WA[:, j * 2048 + i * 512: j * 2048 + (i + 1) * 512],
                csb[i % 2][:], ALU.subtract),
                reads=["WA", f"csb{i % 2}"], writes=["WA"], tag=f"wac{i}.{j}")
    pairs = [(ones_col[:], lrw_bf[:, j:j + 1]) for j in range(KC)]
    mm_group("bT1", psT[1][0:1, 0:1], pairs, ["ones_col", "lrw_bf"], "lrcs")
    p.op("vector", lambda e: e.tensor_scalar_mul(
        rowa[:][0:1, 0:1], psT[1][0:1, 0:1], 1.0 / C),
        reads=["bT1"], writes=["r_m"], tag="lrcse")
    p.op("vector", lambda e: e.tensor_copy(csb[0][:][0:1, 0:1], rowa[:][0:1, 0:1]),
         reads=["r_m", "csb0"], writes=["csb0"], tag="lrcsb")
    p.op("tensor", lambda e: e.matmul(psT[1][:, 0:1], ones_row[:],
                                      csb[0][:][0:1, 0:1],
                                      start=True, stop=True, skip_group_check=True),
         reads=["ones_row", "csb0"], writes=["bT1"], tag="lrcb")
    p.op("scalar", lambda e: e.copy(lrm_col[:], psT[1][:, 0:1]),
         reads=["bT1"], writes=["lrm_col"], tag="lrcbe")
    p.op("vector", lambda e: e.tensor_scalar(
        lrw_bf[:], lrw_bf[:], lrm_col[:], None, ALU.subtract),
        reads=["lrw_bf", "lrm_col"], writes=["lrw_bf"], tag="lrc_sub")
    dlr = alloc("dlr", [128, 8], BF16, ctxA)
    p.op("vector", lambda e: e.tensor_tensor(dlr[:], sb["dn_b"][:], sb["lrw_f"][:], ALU.mult),
         reads=["dn_b", "lrw_f"], writes=["dlr"], tag="dlr")
    pairs = [(dlr[:, j:j + 1], ones_col[:]) for j in range(KC)]
    mm_group("bT1", psT[1][0:1, 0:1], pairs, ["dlr", "ones_col"], "lrb_eff")
    p.op("vector", lambda e: e.tensor_tensor(rowa[:][0:1, 0:1], psT[1][0:1, 0:1],
                                             lrb_sb[:], ALU.add),
         reads=["bT1", "lrb"], writes=["r_m"], tag="lrb_eff_add")
    lrb_bf = alloc("lrb_bf", [1, 1], BF16, ctxA)
    p.op("vector", lambda e: e.tensor_copy(lrb_bf[:], rowa[:][0:1, 0:1]),
         reads=["r_m"], writes=["lrb_bf"])
    p.op("tensor", lambda e: e.matmul(psT[1][:, 0:1], ones_row[:], lrb_bf[:],
                                      start=True, stop=True, skip_group_check=True),
         reads=["ones_row", "lrb_bf"], writes=["bT1"], tag="lrb_bc")
    p.op("scalar", lambda e: e.copy(lrb_col[:], psT[1][:, 0:1]),
         reads=["bT1"], writes=["lrb_col"], tag="lrb_col")

    AU = [(n, s) for n in range(NSEQ) for s in range(NST)]

    def a_load(u):
        n, s = u
        p.op("gpsimd", lambda e, n=n, s=s: e.dma_start(
            out=xa[:].rearrange("p (j t) -> p j t", t=512),
            in_=xT3[:, n * KC:(n + 1) * KC, s * ST:s * ST + 512]),
            writes=["xa"], dma=True, tag=f"ldxA{n}.{s}")

    def a_stats(u):
        n, s = u
        tg = f"A{n}s{s}"
        up = (n * NST + s) % 2
        p.op("scalar", lambda e: e.activation(z2a[:], xa[:], AF.Square),
             reads=["xa"], writes=["z2a"], tag="sq" + tg)
        mm_group("bT0", psT[0][0:1, 0:512],
                 [(ones_col[:], xa[:, j * 512:(j + 1) * 512]) for j in range(KC)],
                 ["ones_col", "xa"], "ssx" + tg)
        mm_group("bT1", psT[1][0:1, 0:512],
                 [(ones_col[:], z2a[:, j * 512:(j + 1) * 512]) for j in range(KC)],
                 ["ones_col", "z2a"], "ssq" + tg)
        ln_stat_rows("bT0", "bT1", psT[0], psT[1],
                     m_out=(rowpA[0:1, 0:512], "rowpA"),
                     rs_out=(rowpA[0:1, 512:1024], "rowpA"),
                     rs12_out=(rowpA[0:1, 1024:1536], "rowpA"), tag="a" + tg)
        p.op("sync", lambda e, n=n, s=s: e.dma_start(
            out=rows_hbm[0:1, (n * NST + s) * 2048:(n * NST + s + 1) * 2048],
            in_=rowpA[:]), reads=["rowpA"], dma=True, tag="strow" + tg)

    def emit_ktr(sub, tg):
        pbf = psT[0][:].bitcast(BF16)

        def trf(e, sub=sub, pbf=pbf):
            ins = None
            for j in range(KC):
                ins = e.transpose(
                    pbf[:, j * 128:(j + 1) * 128],
                    k_tm[:, sub * 1024 + j * 128: sub * 1024 + (j + 1) * 128],
                    ident_bf[:])
            return ins
        p.op("tensor", trf, reads=["k_tm", "ident"], writes=["bT0"],
             tag=f"ktr{sub}" + tg)
        p.op("scalar", lambda e, sub=sub, pbf=pbf: e.copy(
            k_fm[:, sub * 1024:(sub + 1) * 1024], pbf[:]),
            reads=["bT0"], writes=["k_fm"], tag=f"ktre{sub}" + tg)

    def a_proj(u):
        n, s = u
        tg = f"A{n}s{s}"
        up = (n * NST + s) % 2
        ares = f"alpha{up}"
        for sub in range(CH):
            acol = alpha_col[:, up * CH + sub: up * CH + sub + 1]
            for h in range(2):
                pb, pr = ROT4[(sub * 2 + h) % 3]
                pairs = [(xa[:, j * 512 + sub * 128: j * 512 + (sub + 1) * 128],
                          WA[:, j * 2048 + 1024 + h * 512: j * 2048 + 1024 + (h + 1) * 512])
                         for j in range(KC)]
                mm_group(pr, pb[:], pairs, ["xa", "WA"], f"vmm{sub}.{h}" + tg)
                if sub == 0 and h == 0:
                    pbf1 = psT[1][:].bitcast(BF16)

                    def tra(e, pbf1=pbf1):
                        ins = None
                        for c in range(CH):
                            ins = e.transpose(
                                pbf1[:, 2 * c:2 * c + 1],
                                rowpA[0:1, 1024 + c * 128:1024 + (c + 1) * 128],
                                ones_row[0:1, 0:1])
                        return ins
                    p.op("tensor", tra, reads=["rowpA", "ones_row"],
                         writes=["bT1"], tag="atr" + tg)
                    src = pbf1[:, 0:2 * CH].rearrange(
                        "p (c r) -> p c r", r=2)[:, :, 0:1]
                    p.op("vector", lambda e, up=up, src=src: e.tensor_copy(
                        alpha_col[:, up * CH:(up + 1) * CH], src),
                        reads=["bT1"], writes=[f"alpha{up}"], tag="acol" + tg)
                p.op("vector", lambda e, sub=sub, h=h, pb=pb, acol=acol:
                     e.scalar_tensor_tensor(
                         v_tm[:, sub * 1024 + h * 512: sub * 1024 + (h + 1) * 512],
                         pb[:], acol, gbv_t[:, h * 512:(h + 1) * 512],
                         ALU.mult, ALU.add),
                     reads=[pr, ares, "gbv_t"], writes=["v_tm"],
                     tag=f"vev{sub}.{h}" + tg)
            for h in range(2):
                pb, pr = ROT4[(sub * 2 + h + 2) % 3]
                pairs = [(xa[:, j * 512 + sub * 128: j * 512 + (sub + 1) * 128],
                          WA[:, j * 2048 + h * 512: j * 2048 + (h + 1) * 512])
                         for j in range(KC)]
                mm_group(pr, pb[:], pairs, ["xa", "WA"], f"kmm{sub}.{h}" + tg)
                p.op("vector", lambda e, sub=sub, h=h, pb=pb, acol=acol:
                     e.scalar_tensor_tensor(
                         k_tm[:, sub * 1024 + h * 512: sub * 1024 + (h + 1) * 512],
                         pb[:], acol, gbk_t[:, h * 512:(h + 1) * 512],
                         ALU.mult, ALU.add),
                     reads=[pr, ares, "gbk_t"], writes=["k_tm"],
                     tag=f"kev{sub}.{h}" + tg)
            pairs = [(xa[:, j * 512 + sub * 128: j * 512 + (sub + 1) * 128],
                      lrw_bf[:, j:j + 1]) for j in range(KC)]
            mm_group("bS0", psS[0][:, 0:1], pairs, ["xa", "lrw_bf"], f"lr{sub}" + tg)
            p.op("vector", lambda e, sub=sub, acol=acol: e.tensor_scalar(
                lrc[:, sub:sub + 1], psS[0][:, 0:1], acol, lrb_col[:],
                ALU.mult, ALU.add),
                reads=["bS0", ares, "lrb_col"], writes=["lrc"], tag=f"lrev{sub}" + tg)
            kp = sub % 2
            p.op("scalar", lambda e, sub=sub, kp=kp: e.activation(
                ksq[kp][:], k_tm[:, sub * 1024:(sub + 1) * 1024], AF.Square,
                accum_out=ksum[:, sub:sub + 1]),
                reads=["k_tm"], writes=[f"ksq{kp}", "ksum"], tag=f"ksq{sub}" + tg)
            p.op("scalar", lambda e, sub=sub: e.activation(
                krs[:, sub:sub + 1], ksum[:, sub:sub + 1], AF.Sqrt),
                reads=["ksum"], writes=["krs"], tag=f"ksd{sub}" + tg)
            p.op("vector", lambda e, sub=sub: e.reciprocal(
                krs[:, sub:sub + 1], krs[:, sub:sub + 1]),
                reads=["krs"], writes=["krs"], tag=f"krec{sub}" + tg)
            p.op("vector", lambda e, sub=sub: e.tensor_scalar_mul(
                k_tm[:, sub * 1024:(sub + 1) * 1024],
                k_tm[:, sub * 1024:(sub + 1) * 1024], krs[:, sub:sub + 1]),
                reads=["k_tm", "krs"], writes=["k_tm"], tag=f"knrm{sub}" + tg)
            if sub >= 1:
                emit_ktr(sub - 1, tg)

    def a_tbuild_levels(u, cc):
        n, s = u
        tg = f"A{n}s{s}"
        par = cc % 2
        bx, bz, ba, bab = tbX[par], tbZ[par], tbA[par], tbAb[cc]
        rx, rz, ra, rab = f"tbX{par}", f"tbZ{par}", f"tbA{par}", f"tbAb{cc}"
        pT0, pT1 = (psT[0], psT[1]) if par == 0 else (psS[0], psS[1])
        rT0, rT1 = ("bT0", "bT1") if par == 0 else ("bS0", "bS1")

        def lev0():
            pairs = [(k_fm[:, cc * 1024 + j * 128: cc * 1024 + (j + 1) * 128],
                      k_fm[:, cc * 1024 + j * 128: cc * 1024 + (j + 1) * 128])
                     for j in range(KC)]
            mm_group(rT0, pT0[:, 0:128], pairs, ["k_fm"], f"kk{cc}" + tg)
            p.op("vector", lambda e: e.tensor_tensor(
                bx[:], pT0[:, 0:128], maskU[:], ALU.mult),
                reads=[rT0, "maskU"], writes=[rx], tag=f"tX0{cc}" + tg)
            p.op("vector", lambda e: e.tensor_tensor(
                bz[:], pT0[:, 0:128], maskL[:], ALU.mult),
                reads=[rT0, "maskL"], writes=[rz], tag=f"tZ0{cc}" + tg)
            p.op("vector", lambda e: e.tensor_tensor(
                ba[:], ident_bf[:], bx[:], ALU.subtract),
                reads=["ident", rx], writes=[ra], tag=f"tA0{cc}" + tg)

        ev_eng = "scalar" if par == 0 else "vector"

        def mklev(jj):
            def lev():
                if jj < 6:
                    mm_group(rT1, pT1[:, 0:128], [(bz[:], bx[:])], [rz, rx],
                             f"tXn{cc}.{jj}" + tg)
                mm_group(rT0, pT0[:, 128:256], [(bx[:], bz[:])], [rx, rz],
                         f"tZn{cc}.{jj}" + tg)
                cpf = (lambda e, o, i: e.copy(o, i)) if ev_eng == "scalar" \
                    else (lambda e, o, i: e.tensor_copy(o, i))
                if jj < 6:
                    p.op(ev_eng, lambda e, cpf=cpf: cpf(e, bx[:], pT1[:, 0:128]),
                         reads=[rT1], writes=[rx], tag=f"tXe{cc}.{jj}" + tg)
                p.op(ev_eng, lambda e, cpf=cpf: cpf(e, bz[:], pT0[:, 128:256]),
                     reads=[rT0], writes=[rz], tag=f"tZe{cc}.{jj}" + tg)
                mm_group(rT1, pT1[:, 128:256], [(bz[:], ba[:])], [rz, ra],
                         f"tAm{cc}.{jj}" + tg)
                p.op("vector", lambda e: e.tensor_tensor(
                    ba[:], pT1[:, 128:256], ba[:], ALU.add),
                    reads=[rT1, ra], writes=[ra], tag=f"tAe{cc}.{jj}" + tg)
            return lev

        def levf():
            p.op("vector", lambda e: e.tensor_copy(bab[:], ba[:]),
                 reads=[ra], writes=[rab], tag=f"tAc{cc}" + tg)
        return [lev0] + [mklev(jj) for jj in range(1, 7)] + [levf]

    def a_tbuild_pair(u, cc0, cc1=None, post_lev0=None):
        la = a_tbuild_levels(u, cc0)
        lb = a_tbuild_levels(u, cc1) if cc1 is not None else []
        for i in range(max(len(la), len(lb))):
            if i < len(la):
                la[i]()
            if i < len(lb):
                lb[i]()
            if i == 0 and post_lev0 is not None:
                post_lev0()

    def a_chunks(u):
        n, s = u
        tg = f"A{n}s{s}"
        a_tbuild_pair(u, 0, 1, post_lev0=lambda: emit_ktr(3, tg))
        for cc in range(CH):
            gc = s * CH + cc
            halves_started = [False, False]
            any_p = False
            if gc >= CH:
                for h in range(2):
                    pairs = [(k_fm[:, cc * 1024 + j * 128: cc * 1024 + (j + 1) * 128],
                              Sbf[:, j * 1024 + h * 512: j * 1024 + (h + 1) * 512])
                             for j in range(KC)]
                    pbp, prp = (psB[0], "bB0") if h == 0 else (psA[1], "bA1")
                    mm_group(prp, pbp[:], pairs,
                             ["k_fm", "Sbf"], f"pks{cc}.{h}" + tg,
                             start=True, stop=False)
                    halves_started[h] = True
                any_p = True
            for cc2 in range(cc):
                dpar = cc2 % 2
                pairs = [(k_fm[:, cc2 * 1024 + j * 128: cc2 * 1024 + (j + 1) * 128],
                          k_fm[:, cc * 1024 + j * 128: cc * 1024 + (j + 1) * 128])
                         for j in range(KC)]
                mm_group("bA0", psA[0][:, 0:128], pairs, ["k_fm"],
                         f"kkx{cc}.{cc2}" + tg)
                p.op("scalar", lambda e, dpar=dpar: e.copy(tbD[dpar][:], psA[0][:, 0:128]),
                     reads=["bA0"], writes=[f"tbD{dpar}"], tag=f"kkxe{cc}.{cc2}" + tg)
                for h in range(2):
                    pbp, prp = (psB[0], "bB0") if h == 0 else (psA[1], "bA1")
                    mm_group(prp, pbp[:],
                             [(tbD[dpar][:],
                               u_tm[:, cc2 * 1024 + h * 512: cc2 * 1024 + (h + 1) * 512])],
                             [f"tbD{dpar}", "u_tm"], f"pxu{cc}.{cc2}.{h}" + tg,
                             start=not halves_started[h], stop=False)
                    halves_started[h] = True
                any_p = True
            if any_p:
                for h in range(2):
                    pbp, prp = (psB[0], "bB0") if h == 0 else (psA[1], "bA1")
                    p.op("vector", lambda e, cc=cc, h=h, pbp=pbp: e.tensor_tensor(
                        Rbf[:, h * 512:(h + 1) * 512],
                        v_tm[:, cc * 1024 + h * 512: cc * 1024 + (h + 1) * 512],
                        pbp[:], ALU.subtract),
                        reads=["v_tm", prp], writes=["Rbf"], tag=f"R{cc}.{h}" + tg)
                r_ap, r_res = Rbf[:], "Rbf"
            else:
                r_ap, r_res = v_tm[:, cc * 1024:(cc + 1) * 1024], "v_tm"
            if cc == 0:
                a_tbuild_pair(u, 2, 3)
            for h in range(2):
                mm_group(f"bA{h}", psA[h][:],
                         [(tbAb[cc][:], r_ap[:, h * 512:(h + 1) * 512])],
                         [f"tbAb{cc}", r_res], f"u{cc}.{h}" + tg)
                p.op("vector", lambda e, cc=cc, h=h: e.tensor_copy(
                    u_tm[:, cc * 1024 + h * 512: cc * 1024 + (h + 1) * 512],
                    psA[h][:]),
                    reads=[f"bA{h}"], writes=["u_tm"], tag=f"ue{cc}.{h}" + tg)
            p.op("vector", lambda e, cc=cc: e.tensor_scalar_mul(
                ulr_tm[:, cc * 1024:(cc + 1) * 1024],
                u_tm[:, cc * 1024:(cc + 1) * 1024], lrc[:, cc:cc + 1]),
                reads=["u_tm", "lrc"], writes=["ulr_tm"], tag=f"ulr{cc}" + tg)

    def a_state(u):
        n, s = u
        tg = f"A{n}s{s}"
        targets = [] if s == NST - 1 else [("Sf", u_tm, "u_tm", Sf)]
        targets.append(("Wfr", ulr_tm, "ulr_tm", Wf))
        for res_nm, rhs_t, rhs_res, tgt in targets:
            for mj in range(KC):
                for h in range(2):
                    pb, pr = ROTS[(mj * 2 + h) % 4]
                    pairs = [(k_tm[:, cc * 1024 + mj * 128: cc * 1024 + (mj + 1) * 128],
                              rhs_t[:, cc * 1024 + h * 512: cc * 1024 + (h + 1) * 512])
                             for cc in range(CH)]
                    mm_group(pr, pb[:], pairs,
                             ["k_tm", rhs_res], f"up{res_nm}{mj}.{h}" + tg)
                    dst = tgt[:, mj * 1024 + h * 512: mj * 1024 + (h + 1) * 512]
                    if s == 0:
                        p.op("vector", lambda e, dst=dst, pb=pb: e.tensor_copy(
                            dst, pb[:]),
                            reads=[pr], writes=[res_nm],
                            tag=f"upc{res_nm}{mj}.{h}" + tg)
                    else:
                        p.op("vector", lambda e, dst=dst, pb=pb: e.tensor_tensor(
                            dst, pb[:], dst, ALU.add),
                            reads=[pr, res_nm], writes=[res_nm],
                            tag=f"upa{res_nm}{mj}.{h}" + tg)
        if s < NST - 1:
            p.op("vector", lambda e: e.tensor_copy(Sbf[:], Sf[:]),
                 reads=["Sf"], writes=["Sbf"], tag="sbf" + tg)

    def a_wstash(n):
        sfx = f"A{n}"
        p.op("vector", lambda e: e.tensor_copy(Sbf[:], Wf[:]),
             reads=["Wfr"], writes=["Sbf"], tag=f"wbf{sfx}")
        p.op("sync", lambda e, n=n: e.dma_start(
            out=Wstash[:, n * 8192:(n + 1) * 8192], in_=Sbf[:]),
            reads=["Sbf"], writes=[f"Wst{n}"], dma=True, tag=f"wst{sfx}")
        if debug and n == 0:
            for nm, t, rs in (("k_tm", k_tm, "k_tm"), ("v_tm", v_tm, "v_tm"),
                              ("u_tm", u_tm, "u_tm"), ("lrc", lrc, "lrc"),
                              ("Wf", Wf, "Wfr")):
                tp = tap(nm, [128, t[:].shape[1]], F32 if nm in ("lrc", "Wf") else BF16)
                p.op("sync", lambda e, tp=tp, t=t: e.dma_start(out=tp, in_=t[:]),
                     reads=[rs], dma=True, tag="tap_" + nm)

    a_load(AU[0])
    a_stats(AU[0])
    for i, u in enumerate(AU):
        a_proj(u)
        a_chunks(u)
        if i + 1 < len(AU):
            a_load(AU[i + 1])
            a_stats(AU[i + 1])
        a_state(u)
        if u[1] == NST - 1:
            a_wstash(u[0])
    ctxA.close()
    p.barrier()

    ctxB = contextlib.ExitStack()
    Wq = alloc("Wq", [128, KC * 1024], BF16, ctxB)
    Wo = alloc("Wo", [128, KC * 1024], BF16, ctxB)
    W1 = alloc("W1", [128, KC * 1024], BF16, ctxB)
    W2 = alloc("W2", [128, KC * 1024], BF16, ctxB)
    Wcur = alloc("Wcur", [128, 8192], BF16, ctxB)
    xb0 = alloc("xb0", [128, KC * 512], F32, ctxB)
    xb1 = alloc("xb1", [128, KC * 512], F32, ctxB)
    xcb = alloc("xcb", [128, KC * 512], BF16, ctxB)
    qfb = alloc("qfb", [128, KC * 512], BF16, ctxB)
    yf = alloc("yf", [128, KC * 512], BF16, ctxB)
    cf = alloc("cf", [128, KC * 512], F32, ctxB)
    sqb = alloc("sqb", [128, KC * 512], BF16, ctxB)
    t_al = [alloc(f"t_al{pp}", [128, 512], BF16, ctxB) for pp in range(2)]
    t_rs1 = alloc("t_rs1", [128, 512], BF16, ctxB)
    t_mrs1 = alloc("t_mrs1", [128, 512], BF16, ctxB)
    t_rs3 = alloc("t_rs3", [128, 512], BF16, ctxB)
    t_rs4 = alloc("t_rs4", [128, 512], BF16, ctxB)
    t_mrs4 = alloc("t_mrs4", [128, 512], BF16, ctxB)
    z3b = alloc("z3b", [128, KC * 512], BF16, ctxB)
    rowpB = [alloc(f"rowpB{i}", [1, 2048], BF16, ctxB) for i in range(2)]
    rowbf = alloc("rowbf", [1, 1024], BF16, ctxB)
    biasq_bf = alloc("biasq_bf", [128, 8], BF16, ctxB)
    ybias_col = alloc("ybias_col", [128, 8], F32, ctxB)
    browq = rowa[:][0:1, 0:1024]
    BROWQ_KEYS = ["r_m", "r_e2"]

    p.op("gpsimd", lambda e: e.dma_start(
        out=Wq[:].rearrange("p (j f) -> p j f", f=1024), in_=gw3[:, :, 0:1024]),
        writes=["Wq"], dma=True, tag="ld_Wq")
    for qeng, (nm, t, src) in zip(("gpsimd", "gpsimd", "gpsimd"),
                                  (("Wo", Wo, out_w), ("W1", W1, w1),
                                   ("W2", W2, w2))):
        r3 = src.rearrange("(j p) f -> p j f", p=128)
        p.op(qeng, lambda e, t=t, r3=r3: e.dma_start(
            out=t[:].rearrange("p (j f) -> p j f", f=1024), in_=r3),
            writes=[nm], dma=True, tag="ld_" + nm)
    for rown, wt, wres, bvec, bres in (("browq_q", Wq, "Wq", dnb_bf, "dnb_bf"),
                                       ("browq_1", W1, "W1", mnb_bf, "mnb_bf")):
        for i in range(2):
            pairs = [(bvec[:, j:j + 1], wt[:, j * 1024 + i * 512: j * 1024 + (i + 1) * 512])
                     for j in range(KC)]
            mm_group("bT0", psT[0][0:1, 0:512], pairs, [bres, wres], f"{rown}{i}")
            p.op("vector", lambda e, i=i: e.tensor_copy(
                browq[0:1, i * 512:(i + 1) * 512], psT[0][0:1, 0:512]),
                reads=["bT0"], writes=BROWQ_KEYS, tag=f"{rown}e{i}")
        tcol = biasq_col if rown == "browq_q" else bias1_col
        tcol_res = "biasq_col" if rown == "browq_q" else "bias1_col"
        hostb = sb["gb_col"][:, 0:8] if rown == "browq_q" else sb["b1a"][:]
        hostb_res = "gb_col" if rown == "browq_q" else "b1a"

        def trrow(e, browq=browq):
            ins = None
            for j in range(KC):
                ins = e.transpose(psT[1][:, j:j + 1], browq[0:1, j * 128:(j + 1) * 128],
                                  one1_f[:])
            return ins
        p.op("tensor", trrow, reads=BROWQ_KEYS + ["one1"], writes=["bT1"], tag=rown + "tr")
        p.op("vector", lambda e, tcol=tcol, hostb=hostb: e.tensor_tensor(
            tcol[:], psT[1][:, 0:8], hostb, ALU.add),
            reads=["bT1", hostb_res], writes=[tcol_res], tag=rown + "col")
    p.op("vector", lambda e: e.tensor_copy(biasq_bf[:], biasq_col[:]),
         reads=["biasq_col"], writes=["biasq_bf"], tag="biasq_bf")
    for j in range(KC):
        p.op("vector", lambda e, j=j: e.tensor_scalar_mul(
            Wq[:, j * 1024:(j + 1) * 1024], Wq[:, j * 1024:(j + 1) * 1024],
            sb["dn_g"][:, j:j + 1]),
            reads=["Wq", "dn_g"], writes=["Wq"], tag=f"foldWq{j}")
        p.op("vector", lambda e, j=j: e.tensor_scalar_mul(
            W1[:, j * 1024:(j + 1) * 1024], W1[:, j * 1024:(j + 1) * 1024],
            sb["mn_g"][:, j:j + 1]),
            reads=["W1", "mn_g"], writes=["W1"], tag=f"foldW1{j}")
    for wt, wres in ((Wq, "Wq"), (W1, "W1")):
        for i in range(2):
            pairs = [(ones_col[:], wt[:, j * 1024 + i * 512: j * 1024 + (i + 1) * 512])
                     for j in range(KC)]
            mm_group("bT0", psT[0][0:1, 0:512], pairs, ["ones_col", wres],
                     f"{wres}cs{i}")
            p.op("vector", lambda e, i=i: e.tensor_scalar_mul(
                browq[0:1, i * 512:(i + 1) * 512], psT[0][0:1, 0:512], 1.0 / C),
                reads=["bT0"], writes=BROWQ_KEYS, tag=f"{wres}cse{i}")
        p.op("vector", lambda e: e.tensor_copy(rowbf[:], browq),
             reads=BROWQ_KEYS, writes=["rowbf"], tag=f"{wres}csbf")
        prep_tiles = [(t_rs3, "t_rs3"), (t_rs4, "t_rs4")]
        for i in range(2):
            bcast(rowbf[0:1, i * 512:(i + 1) * 512], "rowbf", prep_tiles[i][0][:],
                  prep_tiles[i][1], "bT0", psT[0], f"{wres}cb{i}")
        for j in range(KC):
            for i in range(2):
                p.op("vector", lambda e, j=j, i=i, wt=wt: e.tensor_tensor(
                    wt[:, j * 1024 + i * 512: j * 1024 + (i + 1) * 512],
                    wt[:, j * 1024 + i * 512: j * 1024 + (i + 1) * 512],
                    prep_tiles[i][0][:], ALU.subtract),
                    reads=[wres, prep_tiles[i][1]], writes=[wres],
                    tag=f"{wres}c{j}.{i}")

    B1U = [(n, s) for n in range(NSEQ) for s in range(NST)]

    def b1_ldrow(u):
        n, s = u
        up = (n * NST + s) % 2
        p.op("sync", lambda e, n=n, s=s, up=up: e.dma_start(
            out=rowpB[up][:],
            in_=rows_hbm[0:1, (n * NST + s) * 2048:(n * NST + s + 1) * 2048]),
            writes=[f"rowpB{up}"], dma=True, tag=f"ldrowB1n{n}s{s}")

    def b1_load(u):
        n, s = u
        b1_ldrow(u)
        xb = (xb0, xb1)[s % 2]
        p.op("sync", lambda e, n=n, s=s, xb=xb: e.dma_start(
            out=xb[:].rearrange("p (j t) -> p j t", t=512),
            in_=xT3[:, n * KC:(n + 1) * KC, s * ST:s * ST + 512]),
            writes=[f"xb{s % 2}"], dma=True, tag=f"ldxB1n{n}s{s}")
        p.op("gpsimd", lambda e, n=n, s=s: e.dma_start(
            out=xcb[:].rearrange("p (j t) -> p j t", t=512),
            in_=xT3[:, n * KC:(n + 1) * KC, s * ST:s * ST + 512]),
            writes=["xcb"], dma=True, tag=f"ldxcB1n{n}s{s}")

    def b1_q(u):
        n, s = u
        tg = f"B1n{n}s{s}"
        par = (n * NST + s) % 2
        bcast(rowpB[par][0:1, 1024:1536], f"rowpB{par}", t_al[par][:],
              f"t_al{par}", "bT0", psT[0], "al" + tg)
        for m in range(KC):
            pb, pr = ROT4[m % 3]
            pairs = [(Wq[:, j * 1024 + m * 128: j * 1024 + (m + 1) * 128],
                      xcb[:, j * 512:(j + 1) * 512]) for j in range(KC)]
            mm_group(pr, pb[:], pairs, ["Wq", "xcb"], f"q{m}" + tg)
            p.op("vector", lambda e, m=m, pb=pb, par=par: e.tensor_tensor(
                qfb[:, m * 512:(m + 1) * 512], pb[:], t_al[par][:],
                ALU.mult),
                reads=[pr, f"t_al{par}"], writes=["qfb"], tag=f"qe{m}" + tg)
        if s == 0:
            p.op("sync", lambda e, n=n: e.dma_start(
                out=Wcur[:], in_=Wstash[:, n * 8192:(n + 1) * 8192]),
                reads=[f"Wst{n}"], writes=["Wcur"], dma=True, tag=f"ldWcur{n}")
            for m in range(KC):
                pairs = [(Wcur[:, j * 1024 + m * 128: j * 1024 + (m + 1) * 128],
                          biasq_bf[:, j:j + 1]) for j in range(KC)]
                mm_group("bT1", psT[1][:, m:m + 1], pairs,
                         ["Wcur", "biasq_bf"], f"yb{m}n{n}")
            p.op("vector", lambda e: e.tensor_copy(ybias_col[:], psT[1][:, 0:8]),
                 reads=["bT1"], writes=["ybias_col"], tag=f"ybe{n}")

    def b1_y(u):
        n, s = u
        tg = f"B1n{n}s{s}"
        par = (n * NST + s) % 2
        for m in range(KC):
            pb, pr = ROT4[m % 3]
            pairs = [(Wcur[:, j * 1024 + m * 128: j * 1024 + (m + 1) * 128],
                      qfb[:, j * 512:(j + 1) * 512]) for j in range(KC)]
            mm_group(pr, pb[:], pairs, ["Wcur", "qfb"], f"y{m}" + tg)
            p.op("scalar", lambda e, m=m, pb=pb: e.activation(
                yf[:, m * 512:(m + 1) * 512], pb[:], AF.Identity,
                bias=ybias_col[:, m:m + 1]),
                reads=[pr, "ybias_col"], writes=["yf"], tag=f"ye{m}" + tg)

    def b1_cfinit(u):
        n, s = u
        tg = f"B1n{n}s{s}"
        par = (n * NST + s) % 2
        xr = f"xb{s % 2}"
        xb = (xb0, xb1)[s % 2]
        bcast(rowpB[par][0:1, 512:1024], f"rowpB{par}", t_rs1[:], "t_rs1",
              "bT0", psT[0], "rs1" + tg)
        bcast(rowpB[par][0:1, 0:512], f"rowpB{par}", t_mrs1[:], "t_mrs1",
              "bT1", psT[1], "mrs" + tg)
        p.op("vector", lambda e: e.tensor_tensor(
            t_mrs1[:], t_mrs1[:], t_rs1[:], ALU.mult),
            reads=["t_mrs1", "t_rs1"], writes=["t_mrs1"], tag="mrsm" + tg)
        tt_bcast_split(cf, "cf", xb, xr, t_rs1, "t_rs1", ALU.mult, "am" + tg)
        tt_bcast_split(cf, "cf", cf, "cf", t_mrs1, "t_mrs1", ALU.subtract,
                       "as" + tg)

    def b1_o(u):
        n, s = u
        tg = f"B1n{n}s{s}"
        for m in range(KC):
            pb, pr = ROT4[m % 3]
            pairs = [(Wo[:, j * 1024 + m * 128: j * 1024 + (m + 1) * 128],
                      yf[:, j * 512:(j + 1) * 512]) for j in range(KC)]
            mm_group(pr, pb[:], pairs, ["Wo", "yf"], f"o{m}" + tg)
            p.op("vector", lambda e, m=m, pb=pb: e.scalar_tensor_tensor(
                cf[:, m * 512:(m + 1) * 512], pb[:], sb["ob_col"][:, m:m + 1],
                cf[:, m * 512:(m + 1) * 512], ALU.add, ALU.add),
                reads=[pr, "ob_col", "cf"], writes=["cf"], tag=f"oe{m}" + tg)

    def b1_c(u):
        n, s = u
        tg = f"B1n{n}s{s}"
        if debug and n == 0 and s == 0:
            tp = tap("c", [128, KC * 512])
            p.op("sync", lambda e, tp=tp: e.dma_start(out=tp, in_=cf[:]),
                 reads=["cf"], dma=True, tag="tap_c")
        p.op("vector", lambda e: e.tensor_copy(z3b[:], cf[:]),
             reads=["cf"], writes=["z3b"], tag="ccast" + tg)
        p.op("gpsimd", lambda e: e.tensor_tensor(sqb[:], cf[:], cf[:], ALU.mult),
             reads=["cf"], writes=["sqb"], tag="csq" + tg)
        mm_group("bT0", psT[0][0:1, 0:512],
                 [(ones_col[:], z3b[:, j * 512:(j + 1) * 512]) for j in range(KC)],
                 ["ones_col", "z3b"], "cssx" + tg)
        mm_group("bT1", psT[1][0:1, 0:512],
                 [(ones_col[:], sqb[:, j * 512:(j + 1) * 512]) for j in range(KC)],
                 ["ones_col", "sqb"], "cssq" + tg)
        ln_stat_rows("bT0", "bT1", psT[0], psT[1],
                     rs_out=(rowbf[0:1, 0:512], "rowbf"), tag="cln" + tg)
        bcast(rowbf[0:1, 0:512], "rowbf", t_rs3[:], "t_rs3", "bT0", psT[0],
              "rs3" + tg)

    def b1_h(u):
        n, s = u
        tg = f"B1n{n}s{s}"
        for m in range(KC):
            pb, pr = ROT4[m % 3]
            pairs = [(W1[:, j * 1024 + m * 128: j * 1024 + (m + 1) * 128],
                      z3b[:, j * 512:(j + 1) * 512]) for j in range(KC)]
            mm_group(pr, pb[:], pairs, ["W1", "z3b"], f"h{m}" + tg)
            p.op("vector", lambda e, m=m, pb=pb: e.tensor_tensor(
                sqb[:, m * 512:(m + 1) * 512], pb[:], t_rs3[:], ALU.mult),
                reads=[pr, "t_rs3", "sqb"], writes=["sqb"], tag=f"hm{m}" + tg)
            p.op("scalar", lambda e, m=m: e.activation(
                yf[:, m * 512:(m + 1) * 512], sqb[:, m * 512:(m + 1) * 512],
                AF.Relu, bias=bias1_col[:, m:m + 1]),
                reads=["sqb", "bias1_col"], writes=["yf"], tag=f"he{m}" + tg)

    def b1_g(u):
        n, s = u
        tg = f"B1n{n}s{s}"
        for m in range(KC):
            pb, pr = ROT4[m % 3]
            pairs = [(W2[:, j * 1024 + m * 128: j * 1024 + (m + 1) * 128],
                      yf[:, j * 512:(j + 1) * 512]) for j in range(KC)]
            mm_group(pr, pb[:], pairs, ["W2", "yf"], f"g{m}" + tg)
            p.op("vector", lambda e, m=m, pb=pb: e.scalar_tensor_tensor(
                cf[:, m * 512:(m + 1) * 512], pb[:], sb["b2a"][:, m:m + 1],
                cf[:, m * 512:(m + 1) * 512], ALU.add, ALU.add),
                reads=[pr, "b2a", "cf"], writes=["cf"], tag=f"ge{m}" + tg)

    def b1_gcasts(u):
        n, s = u
        tg = f"B1n{n}s{s}"
        p.op("vector", lambda e: e.tensor_copy(z3b[:], cf[:]),
             reads=["cf"], writes=["z3b"], tag="gcast" + tg)
        p.op("gpsimd", lambda e: e.tensor_tensor(sqb[:], cf[:], cf[:], ALU.mult),
             reads=["cf"], writes=["sqb"], tag="gsq" + tg)

    def b1_l4stats(u):
        n, s = u
        tg = f"B1n{n}s{s}"
        mm_group("bT0", psT[0][0:1, 0:512],
                 [(ones_col[:], z3b[:, j * 512:(j + 1) * 512]) for j in range(KC)],
                 ["ones_col", "z3b"], "gssx" + tg)
        mm_group("bT1", psT[1][0:1, 0:512],
                 [(ones_col[:], sqb[:, j * 512:(j + 1) * 512]) for j in range(KC)],
                 ["ones_col", "sqb"], "gssq" + tg)
        ln_stat_rows("bT0", "bT1", psT[0], psT[1],
                     m_out=(rowbf[0:1, 512:1024], "rowbf"),
                     rs_out=(rowbf[0:1, 0:512], "rowbf"), tag="g4" + tg)

    def b1_l4bcast(u):
        n, s = u
        tg = f"B1n{n}s{s}"
        bcast(rowbf[0:1, 0:512], "rowbf", t_rs4[:], "t_rs4", "bT0", psT[0],
              "rs4" + tg)
        bcast(rowbf[0:1, 512:1024], "rowbf", t_mrs4[:], "t_mrs4", "bT1", psT[1],
              "mrs4" + tg)
        p.op("vector", lambda e: e.tensor_tensor(
            t_mrs4[:], t_mrs4[:], t_rs4[:], ALU.mult),
            reads=["t_mrs4", "t_rs4"], writes=["t_mrs4"], tag="mrs4m" + tg)

    def b1_l4b(u):
        n, s = u
        tg = f"B1n{n}s{s}"
        xr = f"xb{s % 2}"
        xb = (xb0, xb1)[s % 2]
        tt_bcast_split(sqb, "sqb", z3b, "z3b", t_rs4, "t_rs4", ALU.mult,
                       "h1m" + tg)
        tt_bcast_split(sqb, "sqb", sqb, "sqb", t_mrs4, "t_mrs4", ALU.subtract,
                       "h1s" + tg)
        tt_plain_split(xb, xr, xb, xr, sqb, "sqb", ALU.add, "h1a" + tg)
        p.op("sync", lambda e, n=n, s=s, xb=xb: e.dma_start(
            out=h1_hbm[:, (n * NST + s) * 4096:(n * NST + s + 1) * 4096], in_=xb[:]),
            reads=[xr], writes=[f"h1st{n}_{s}"], dma=True, tag="sth1" + tg)
        if debug and n == 0 and s == 0:
            tp = tap("h1", [128, KC * 512])
            p.op("sync", lambda e, tp=tp, xb=xb: e.dma_start(out=tp, in_=xb[:]),
                 reads=[xr], dma=True, tag="tap_h1")

    b1_load(B1U[0])
    b1_q(B1U[0])
    b1_y(B1U[0])
    b1_cfinit(B1U[0])
    for i, u in enumerate(B1U):
        nxt = B1U[i + 1] if i + 1 < len(B1U) else None
        b1_o(u)
        if nxt is not None:
            b1_load(nxt)
            b1_q(nxt)
        b1_c(u)
        b1_h(u)
        b1_g(u)
        b1_gcasts(u)
        if nxt is not None:
            b1_y(nxt)
        b1_l4stats(u)
        if nxt is not None:
            b1_cfinit(nxt)
        b1_l4bcast(u)
        b1_l4b(u)
    ctxB.close()
    p.barrier()

    ctxC = contextlib.ExitStack()
    U1s = alloc("U1s", [128, KC * 256], BF16, ctxC)
    V1s = alloc("V1s", [128, 2 * 4096], BF16, ctxC)
    U2s = alloc("U2s", [128, 32 * 256], BF16, ctxC)
    V2s = alloc("V2s", [128, 2 * 1024], BF16, ctxC)
    h1a = alloc("h1a", [128, KC * 512], F32, ctxC)
    h1b = alloc("h1b", [128, KC * 512], F32, ctxC)
    Hb = alloc("Hb", [128, 32 * 512], BF16, ctxC)
    rb2 = [alloc(f"rb2_{i}", [128, KC * 512], BF16, ctxC) for i in range(2)]
    hc = [alloc(f"hc{i}", [128, KC * 512], BF16, ctxC) for i in range(2)]
    sq2 = alloc("sq2", [128, KC * 512], BF16, ctxC)
    sq6 = alloc("sq6", [128, KC * 512], BF16, ctxC)
    Pbf = [alloc(f"Pbf{i}", [128, 2 * 512], BF16, ctxC) for i in range(2)]
    Qbf = [alloc(f"Qbf{i}", [128, 2 * 512], BF16, ctxC) for i in range(2)]
    t2_rs5 = alloc("t2_rs5", [128, 512], BF16, ctxC)
    t2_rs6 = alloc("t2_rs6", [128, 512], BF16, ctxC)
    t2_m6 = alloc("t2_m6", [128, 512], BF16, ctxC)
    t2_cs = alloc("t2_cs", [128, 256], BF16, ctxC)
    rowbf2 = alloc("rowbf2", [1, 1024], BF16, ctxC)
    brow2 = alloc("brow2", [1, 256], F32, ctxC)

    for qeng, (nm, t, src, w) in zip(
            ("gpsimd", "gpsimd", "gpsimd", "gpsimd"),
            (("U1s", U1s, U1, 256), ("V1s", V1s, V1, 4096),
             ("U2s", U2s, U2, 256), ("V2s", V2s, V2, 1024))):
        r3 = src.rearrange("(j p) f -> p j f", p=128)
        p.op(qeng, lambda e, t=t, r3=r3, w=w: e.dma_start(
            out=t[:].rearrange("p (j f) -> p j f", f=w), in_=r3),
            writes=[nm], dma=True, tag="ld_" + nm)
    pairs = [(ones_col[:], U1s[:, j * 256:(j + 1) * 256]) for j in range(KC)]
    mm_group("bT0", psT[0][0:1, 0:256], pairs, ["ones_col", "U1s"], "u1cs")
    p.op("vector", lambda e: e.tensor_scalar_mul(brow2[:], psT[0][0:1, 0:256], 1.0 / C),
         reads=["bT0"], writes=["brow2"], tag="u1cse")
    p.op("vector", lambda e: e.tensor_copy(rowbf2[0:1, 0:256], brow2[:]),
         reads=["brow2"], writes=["rowbf2"], tag="u1csbf")
    p.op("tensor", lambda e: e.matmul(psT[0][:, 0:256], ones_row[:],
                                      rowbf2[0:1, 0:256], start=True, stop=True,
                                      skip_group_check=True),
         reads=["ones_row", "rowbf2"], writes=["bT0"], tag="u1cb")
    p.op("scalar", lambda e: e.copy(t2_cs[:], psT[0][:, 0:256]),
         reads=["bT0"], writes=["t2_cs"], tag="u1cbe")
    for j in range(KC):
        p.op("vector", lambda e, j=j: e.tensor_tensor(
            U1s[:, j * 256:(j + 1) * 256], U1s[:, j * 256:(j + 1) * 256],
            t2_cs[:], ALU.subtract),
            reads=["U1s", "t2_cs"], writes=["U1s"], tag=f"u1c{j}")

    B2U = [(n, s) for n in range(NSEQ) for s in range(NST)]

    def b2_load(u):
        n, s = u
        hb = (h1a, h1b)[s % 2]
        p.op("sync", lambda e, n=n, s=s, hb=hb: e.dma_start(
            out=hb[:], in_=h1_hbm[:, (n * NST + s) * 4096:(n * NST + s + 1) * 4096]),
            reads=[f"h1st{n}_{s}"], writes=[f"h1{s % 2}"], dma=True,
            tag=f"ldh1B2n{n}s{s}")
        p.op("gpsimd", lambda e, n=n, s=s: e.dma_start(
            out=hc[s % 2][:],
            in_=h1_hbm[:, (n * NST + s) * 4096:(n * NST + s + 1) * 4096]),
            reads=[f"h1st{n}_{s}"], writes=[f"hc{s % 2}"], dma=True,
            tag=f"ldhcB2n{n}s{s}")

    def b2_head(u):
        n, s = u
        tg = f"B2n{n}s{s}"
        hb = (h1a, h1b)[s % 2]
        hr = f"h1{s % 2}"
        hcb, hcr = hc[s % 2], f"hc{s % 2}"
        pp = s % 2
        for m in range(2):
            pairs = [(U1s[:, j * 256 + m * 128: j * 256 + (m + 1) * 128],
                      hcb[:, j * 512:(j + 1) * 512]) for j in range(KC)]
            mm_group(f"bA{m}", psA[m][:], pairs, ["U1s", hcr], f"P{m}" + tg)
        p.op("gpsimd", lambda e, hb=hb: e.tensor_tensor(
            sq2[:], hb[:], hb[:], ALU.mult),
            reads=[hr], writes=["sq2"], tag="h1sq" + tg)
        mm_group("bT0", psT[0][0:1, 0:512],
                 [(ones_col[:], hcb[:, j * 512:(j + 1) * 512]) for j in range(KC)],
                 ["ones_col", hcr], "h1ssx" + tg)
        mm_group("bT1", psT[1][0:1, 0:512],
                 [(ones_col[:], sq2[:, j * 512:(j + 1) * 512]) for j in range(KC)],
                 ["ones_col", "sq2"], "h1ssq" + tg)
        ln_stat_rows("bT0", "bT1", psT[0], psT[1],
                     rs_out=(rowbf2[0:1, 0:512], "rowbf2"), tag="l5" + tg)
        bcast(rowbf2[0:1, 0:512], "rowbf2", t2_rs5[:], "t2_rs5", "bT0", psT[0],
              "rs5" + tg)
        for m in range(2):
            p.op("vector", lambda e, m=m, pp=pp: e.tensor_tensor(
                Pbf[pp][:, m * 512:(m + 1) * 512], psA[m][:], t2_rs5[:],
                ALU.mult),
                reads=[f"bA{m}", "t2_rs5"], writes=[f"Pbf{pp}"], tag=f"Pe{m}" + tg)

    ROTH = [(psB[0], "bB0"), (psT[0], "bT0"), (psT[1], "bT1")]

    def b2_body(u):
        n, s = u
        tg = f"B2n{n}s{s}"
        pp = s % 2
        DEPTH = 4
        def qmm(mh):
            for mq in range(2):
                mm_group(f"bS{mq}", psS[mq][:],
                         [(U2s[:, mh * 256 + mq * 128: mh * 256 + (mq + 1) * 128],
                           Hb[:, mh * 512:(mh + 1) * 512])],
                         ["U2s", f"Hb{mh % 8}"], f"Q{mh}.{mq}" + tg,
                         start=(mh == 0), stop=(mh == 31))
        for mh in range(32):
            pb, pr = ROTH[mh % 3]
            pairs = [(V1s[:, j * 4096 + mh * 128: j * 4096 + (mh + 1) * 128],
                      Pbf[pp][:, j * 512:(j + 1) * 512]) for j in range(2)]
            mm_group(pr, pb[:], pairs, ["V1s", f"Pbf{pp}"], f"H{mh}" + tg)
            p.op("scalar", lambda e, mh=mh, pb=pb: e.activation(
                Hb[:, mh * 512:(mh + 1) * 512], pb[:], AF.Gelu_apprx_tanh,
                bias=sb["fb1"][:, mh:mh + 1]),
                reads=[pr, "fb1"], writes=[f"Hb{mh % 8}"], tag=f"He{mh}" + tg)
            if mh >= DEPTH:
                qmm(mh - DEPTH)
        for mh in range(32 - DEPTH, 32):
            qmm(mh)
        for mq in range(2):
            p.op("vector", lambda e, mq=mq, pp=pp: e.tensor_copy(
                Qbf[pp][:, mq * 512:(mq + 1) * 512], psS[mq][:]),
                reads=[f"bS{mq}"], writes=[f"Qbf{pp}"], tag=f"Qe{mq}" + tg)
        for m in range(KC):
            pb, pr = ROTH[m % 3]
            pairs = [(V2s[:, j * 1024 + m * 128: j * 1024 + (m + 1) * 128],
                      Qbf[pp][:, j * 512:(j + 1) * 512]) for j in range(2)]
            mm_group(pr, pb[:], pairs, ["V2s", f"Qbf{pp}"], f"r{m}" + tg)
            p.op("vector", lambda e, m=m, pb=pb, pp=pp: e.tensor_scalar_add(
                rb2[pp][:, m * 512:(m + 1) * 512], pb[:],
                sb["fb2"][:, m:m + 1]),
                reads=[pr, "fb2"], writes=[f"rb2_{pp}"], tag=f"re{m}" + tg)

    def b2_tail(u):
        n, s = u
        tg = f"B2n{n}s{s}"
        pp = s % 2
        hb = (h1a, h1b)[s % 2]
        hr = f"h1{s % 2}"
        rb = rb2[pp]
        rr = f"rb2_{pp}"
        if debug and n == 0 and s == 0:
            tp = tap("r", [128, KC * 512], BF16)
            p.op("sync", lambda e, tp=tp: e.dma_start(out=tp, in_=rb[:]),
                 reads=[rr], dma=True, tag="tap_r")
        p.op("gpsimd", lambda e: e.tensor_tensor(sq6[:], rb[:], rb[:], ALU.mult),
             reads=[rr], writes=["sq6"], tag="rsq" + tg)
        mm_group("bT0", psT[0][0:1, 0:512],
                 [(ones_col[:], rb[:, j * 512:(j + 1) * 512]) for j in range(KC)],
                 ["ones_col", rr], "rssx" + tg)
        mm_group("bT1", psT[1][0:1, 0:512],
                 [(ones_col[:], sq6[:, j * 512:(j + 1) * 512]) for j in range(KC)],
                 ["ones_col", "sq6"], "rssq" + tg)
        ln_stat_rows("bT0", "bT1", psT[0], psT[1],
                     m_out=(rowbf2[0:1, 512:1024], "rowbf2"),
                     rs_out=(rowbf2[0:1, 0:512], "rowbf2"), tag="l6" + tg)
        bcast(rowbf2[0:1, 0:512], "rowbf2", t2_rs6[:], "t2_rs6", "bT0", psT[0],
              "rs6" + tg)
        bcast(rowbf2[0:1, 512:1024], "rowbf2", t2_m6[:], "t2_m6", "bT1", psT[1],
              "mrs6" + tg)
        p.op("vector", lambda e: e.tensor_tensor(
            t2_m6[:], t2_m6[:], t2_rs6[:], ALU.mult),
            reads=["t2_m6", "t2_rs6"], writes=["t2_m6"], tag="mrs6m" + tg)
        tt_bcast_split(rb, rr, rb, rr, t2_rs6, "t2_rs6", ALU.mult,
                       "om" + tg)
        tt_bcast_split(rb, rr, rb, rr, t2_m6, "t2_m6", ALU.subtract,
                       "os" + tg)
        tt_plain_split(hb, hr, hb, hr, rb, rr, ALU.add, "oa" + tg)
        p.op("sync", lambda e, n=n, s=s, hb=hb: e.dma_start(
            out=yT3[:, n * KC:(n + 1) * KC, s * ST:s * ST + 512],
            in_=hb[:].rearrange("p (j t) -> p j t", t=512)),
            reads=[hr], dma=True, tag="sty" + tg)

    b2_load(B2U[0])
    b2_head(B2U[0])
    for i, u in enumerate(B2U):
        nxt = B2U[i + 1] if i + 1 < len(B2U) else None
        b2_body(u)
        if nxt is not None:
            b2_load(nxt)
            b2_head(nxt)
        b2_tail(u)
    ctxC.close()

    p.finalize()
    p.emit(ctx)
    return nc, ctx, taps


def _col(v):
    v = np.asarray(v, dtype=np.float32).reshape(-1)
    return np.ascontiguousarray(v.reshape(-1, 128).T)


def kernel(L0_attn_x, L0_attn_dn_g, L0_attn_dn_b, L0_attn_gates_w, L0_attn_gates_b,
           L0_attn_lr_w, L0_attn_lr_b, L0_attn_out_w, L0_attn_out_b, L0_attn_mn_g,
           L0_attn_mn_b, L0_attn_w1, L0_attn_b1, L0_attn_w2, L0_attn_b2,
           L0_ffn_U1, L0_ffn_V1, L0_ffn_b1, L0_ffn_U2, L0_ffn_V2, L0_ffn_b2,
           _debug=False):
    x = np.asarray(L0_attn_x, dtype=np.float32)
    N = x.shape[0]
    assert x.shape == (N, T, C) and N == NCORE * NSEQ
    xT = np.ascontiguousarray(x.transpose(0, 2, 1))

    f32 = lambda a: np.ascontiguousarray(np.asarray(a, dtype=np.float32))
    common = {
        "gates_w": f32(L0_attn_gates_w), "out_w": f32(L0_attn_out_w),
        "w1": f32(L0_attn_w1), "w2": f32(L0_attn_w2),
        "U1": f32(L0_ffn_U1), "V1": f32(L0_ffn_V1),
        "U2": f32(L0_ffn_U2), "V2": f32(L0_ffn_V2),
        "dn_g": _col(L0_attn_dn_g), "dn_b": _col(L0_attn_dn_b),
        "mn_g": _col(L0_attn_mn_g), "mn_b": _col(L0_attn_mn_b),
        "gates_b_col": _col(L0_attn_gates_b), "out_b_col": _col(L0_attn_out_b),
        "b1a_col": _col(L0_attn_b1), "b2a_col": _col(L0_attn_b2),
        "fb1_col": _col(L0_ffn_b1), "fb2_col": _col(L0_ffn_b2),
        "lrw_col": _col(L0_attn_lr_w),
        "lrb_in": f32(L0_attn_lr_b).reshape(1, 1),
        "gates_b_row": f32(L0_attn_gates_b).reshape(1, 3 * C),
        "ident_in": np.eye(128, dtype=np.float32),
        "maskL_in": np.tril(np.ones((128, 128), np.float32), -1),
        "maskU_in": np.triu(np.ones((128, 128), np.float32), 1),
    }
    nc, ctx, taps = build_program(debug=_debug)
    in_maps = []
    for i in range(NCORE):
        m = dict(common)
        m["xT"] = np.ascontiguousarray(xT[i * NSEQ:(i + 1) * NSEQ])
        in_maps.append(m)
    res = run_bass_kernel_spmd(nc, in_maps, core_ids=list(range(NCORE)))
    ctx.close()
    out = np.empty((N, T, C), dtype=np.float32)
    for i in range(NCORE):
        yt = np.asarray(res.results[i]["yT"])
        out[i * NSEQ:(i + 1) * NSEQ] = yt.transpose(0, 2, 1)
    if _debug:
        return out, {k: np.asarray(res.results[0]["tap_" + k]) for k in taps}
    return out
